# version 01
# baseline (speedup 1.0000x reference)
import numpy as np
import ml_dtypes
import concourse.bass as bass
import concourse.mybir as mybir
from concourse.bass_utils import run_bass_kernel_spmd

F32 = mybir.dt.float32
BF16 = mybir.dt.bfloat16
AF = mybir.ActivationFunctionType
ALU = mybir.AluOpType
AX = mybir.AxisListType

D = 2048
T = 2048
NH = 8
HD = 256
HALF = 128
DT = D // 128
TT = T // 128
NQB = T // 512
E = 8
TCH = 1024
RMS_EPS = 1.1920929e-07
LN_EPS = 1e-5
LAMBDA_INIT = 0.8

A_OFF = 17408
B_OFF = 82944
C_OFF = 148480
D_OFF = 214656


class Ctr:
    def __init__(self, handle, name):
        self.h = handle
        self.name = name
        self.n = 0

    def inc(self, instr, k=1):
        instr.then_inc(self.h, k)
        self.n += k
        return self.n


def build(n_heads=NH, emit_attn=True, emit_ffn=True, debug=()):
    nc = bass.Bass("TRN2", target_bir_lowering=False, debug=False)

    xt = nc.dram_tensor("xt", [D, T], F32, kind="ExternalInput").ap()
    wqT = nc.dram_tensor("wqT", [D, D], F32, kind="ExternalInput").ap()
    wkT = nc.dram_tensor("wkT", [D, D], F32, kind="ExternalInput").ap()
    wvT = nc.dram_tensor("wvT", [D, D], F32, kind="ExternalInput").ap()
    woT = nc.dram_tensor("woT", [D, D], F32, kind="ExternalInput").ap()
    gk = nc.dram_tensor("gk", [D, E], F32, kind="ExternalInput").ap()
    lam = nc.dram_tensor("lam", [1, 4 * HD], F32, kind="ExternalInput").ap()
    wi0 = nc.dram_tensor("wi0", [E, D, D], F32, kind="ExternalInput").ap()
    wi1 = nc.dram_tensor("wi1", [E, D, D], F32, kind="ExternalInput").ap()
    wom = nc.dram_tensor("wom", [E, D, D], F32, kind="ExternalInput").ap()
    idf = nc.dram_tensor("idf", [128, 128], F32, kind="ExternalInput").ap()
    onesb_in = nc.dram_tensor("onesb_in", [128, 128], F32, kind="ExternalInput").ap()
    onesf_in = nc.dram_tensor("onesf_in", [128, 128], F32, kind="ExternalInput").ap()
    maskf = nc.dram_tensor("maskf", [128, 128], F32, kind="ExternalInput").ap()
    cst_in = nc.dram_tensor("cst_in", [128, 8], F32, kind="ExternalInput").ap()
    esel_in = nc.dram_tensor("esel_in", [8, E * 128], F32, kind="ExternalInput").ap()
    outT = nc.dram_tensor("outT", [D, T], F32, kind="ExternalOutput").ap()
    dbg = {}
    _dbg_shapes = {"d_cw": [E, T], "d_qt": [128, 2 * T], "d_kt": [128, 2 * T],
                   "d_vt": [128, TT * HD], "d_att": [128, 16 * 512], "d_rb": [128, 512],
                   "d_sc": [128, 512], "d_sc0": [128, 512]}
    for name in debug:
        shape = _dbg_shapes.get(name, [D, T])
        dbg[name] = nc.dram_tensor(name, shape, F32, kind="ExternalOutput").ap()

    al = nc.alloc_sbuf_tensor_at
    XT = al("XT", [128, DT * T], BF16, offset=A_OFF)
    RES1 = al("RES1", [128, DT * T], BF16, offset=A_OFF)
    ACC = al("ACC", [128, DT * TCH], F32, offset=A_OFF)
    Y = al("Y", [128, DT * T], BF16, offset=B_OFF)
    X1T = al("X1T", [128, DT * T], BF16, offset=B_OFF)
    QT = al("QT", [128, 2 * T], BF16, offset=C_OFF + 0)
    KT = al("KT", [128, 2 * T], BF16, offset=C_OFF + 8192)
    VTOK = al("VTOK", [128, TT * HD], BF16, offset=C_OFF + 16384)
    ATT = al("ATT", [128, 16 * 512], BF16, offset=C_OFF + 24576)
    SQ = al("SQ", [128, 2 * T], BF16, offset=C_OFF + 24576)
    LAMV = al("LAMV", [1, 4 * HD], F32, offset=C_OFF + 32768)
    WS = [al(f"WS{i}", [128, DT * HD], BF16, offset=C_OFF + 40960 + i * 8192) for i in range(2)]
    TMP = al("TMPB", [128, 512], BF16, offset=C_OFF + 57344)
    RB = al("RB", [128, 512], F32, offset=C_OFF + 58368)
    OXF = [al(f"OXF{i}", [128, T], F32, offset=C_OFF + 0 + i * 8192) for i in range(2)]
    LNSQ = [al(f"LNSQ{i}", [128, T], BF16, offset=C_OFF + 16384 + i * 4096) for i in range(2)]
    LNM = al("LNM", [128, T], F32, offset=C_OFF + 24576)
    LNT = al("LNT", [128, 512], F32, offset=C_OFF + 61440)
    LOGIT = al("LOGIT", [8, T], F32, offset=C_OFF + 32768)
    CWTOK = al("CWTOK", [128, TT * E], F32, offset=C_OFF + 40960)
    H = al("H", [128, DT * TCH], BF16, offset=C_OFF + 0)
    WI0S = [al(f"WI0S{i}", [128, DT * 128], BF16, offset=C_OFF + 32768 + i * 4096) for i in range(2)]
    WI1S = [al(f"WI1S{i}", [128, DT * 128], BF16, offset=C_OFF + 40960 + i * 4096) for i in range(2)]
    WOS = [al(f"WOS{i}", [128, DT * 128], BF16, offset=C_OFF + 49152 + i * 4096) for i in range(2)]
    CWB = al("CWB", [128, TCH], F32, offset=C_OFF + 57344)
    SILU = al("SILU", [128, TCH], BF16, offset=C_OFF + 61440)
    FT2 = al("FT2", [128, TCH], BF16, offset=C_OFF + 63488)
    LNM2 = al("LNM2", [128, TCH], F32, offset=C_OFF + 61440)
    SQ2 = al("SQ2", [128, 2 * TCH], BF16, offset=C_OFF + 0)
    OUTS = [al(f"OUTS{i}", [128, TCH], F32, offset=C_OFF + 4096 + i * 4096) for i in range(2)]

    IDF = al("IDF", [128, 128], F32, offset=D_OFF + 0)
    ONESB = al("ONESB", [128, 128], BF16, offset=D_OFF + 512)
    ONESF = al("ONESF", [128, 128], F32, offset=D_OFF + 768)
    MASK = al("MASK", [128, 128], F32, offset=D_OFF + 1280)
    CST = al("CST", [128, 8], F32, offset=D_OFF + 1792)
    NLAM = al("NLAM", [128, 1], F32, offset=D_OFF + 1824)
    RK = al("RK", [128, 32], F32, offset=D_OFF + 1856)
    GKB = al("GKB", [128, DT * E], BF16, offset=D_OFF + 1984)
    GTMP = al("GTMP", [128, 16], F32, offset=D_OFF + 2240)
    GEQ = [al(f"GEQ{i}", [128, E], F32, offset=D_OFF + 2304 + i * 32) for i in range(3)]
    CWT = al("CWT", [8, T], F32, offset=D_OFF + 2400)
    ESEL = al("ESEL", [8, E * 128], F32, offset=D_OFF + 10592)

    PB = [nc.alloc_psum_tensor(f"PB{i}", [128, 512], F32) for i in range(4)]
    P2 = nc.alloc_psum_tensor("P2", [128, 2048], F32)
    BANKS = ["PB0", "PB1", "PB2", "PB3", "P2a", "P2b", "P2c", "P2d"]
    bank_dep = {b: None for b in BANKS}

    from contextlib import ExitStack
    ctx = ExitStack()
    mk = lambda name: Ctr(ctx.enter_context(nc.semaphore(name)), name)
    LDG = mk("LDG")
    LDS = mk("LDS")
    SWS = [mk("SWS0"), mk("SWS1")]
    SWI0 = [mk("SWI0a"), mk("SWI0b")]
    SWI1 = [mk("SWI1a"), mk("SWI1b")]
    SWO = [mk("SWOa"), mk("SWOb")]
    SOX = [mk("SOXa"), mk("SOXb")]
    SST = [mk("SSTa"), mk("SSTb")]
    DBG = mk("DBG")
    PT = mk("PT")
    PV = mk("PV")
    PS = mk("PS")

    blk = ctx.enter_context(nc.Block())

    known = {k: {} for k in "tvsgy"}

    def wt(eng, kind, dep):
        if dep is None:
            return
        sem, val = dep
        if val <= 0:
            return
        k = known[kind]
        if k.get(sem.name, -1) >= val:
            return
        eng.wait_ge(sem.h, val)
        k[sem.name] = val

    def wait_banks(t, names):
        for b in names:
            wt(t, "t", bank_dep[b])

    def rel_banks(names, dep):
        for b in names:
            bank_dep[b] = dep

    def TS(f):
        blk.tensor(f)

    def VS(f):
        blk.vector(f)

    def SS(f):
        blk.scalar(f)

    def GS(f):
        blk.gpsimd(f)

    def YS(f):
        blk.sync(f)

    def _g(g):
        g.dma_start(out=ONESB[:], in_=onesb_in).then_inc(LDG.h, 16); LDG.n += 16
        g.dma_start(out=GKB[:].rearrange("p (n m) -> p n m", n=DT),
                    in_=gk.rearrange("(n p) m -> p n m", p=128)).then_inc(LDG.h, 16); LDG.n += 16
        for dt in range(DT):
            g.dma_start(out=XT[:, dt * T:(dt + 1) * T],
                        in_=xt[dt * 128:(dt + 1) * 128, :]).then_inc(LDG.h, 16); LDG.n += 16
    GS(_g)
    ld_xt = LDG.n
    ld_gkb = 32

    def _y(y):
        for dst, src in ((IDF, idf), (ONESF, onesf_in), (MASK, maskf), (CST, cst_in),
                         (LAMV, lam), (ESEL, esel_in)):
            y.dma_start(out=dst[:], in_=src).then_inc(LDS.h, 16); LDS.n += 16
    YS(_y)
    ld_consts = LDS.n

    lam_col = None

    def _v(v):
        wt(v, "v", (LDS, ld_consts))
        v.tensor_mul(LNT[0:1, 0:256], LAMV[0:1, 0:HD], LAMV[0:1, HD:2 * HD])
        i = PV.inc(v.tensor_mul(LNT[0:1, 256:512], LAMV[0:1, 2 * HD:3 * HD], LAMV[0:1, 3 * HD:4 * HD]))
        wt(v, "v", (PV, i))
        v.reduce_sum(GTMP[0:1, 0:1], LNT[0:1, 0:256], axis=AX.X)
        _v.out = PV.inc(v.reduce_sum(GTMP[0:1, 1:2], LNT[0:1, 256:512], axis=AX.X))
    VS(_v); v_sums = _v.out

    def _s(s):
        wt(s, "s", (PV, v_sums))
        s.activation(GTMP[0:1, 2:3], GTMP[0:1, 0:1], AF.Exp)
        _s.out = PS.inc(s.activation(GTMP[0:1, 3:4], GTMP[0:1, 1:2], AF.Exp))
    SS(_s); s_exps = _s.out

    def _v(v):
        wt(v, "v", (PS, s_exps))
        i = PV.inc(v.tensor_sub(GTMP[0:1, 4:5], GTMP[0:1, 3:4], GTMP[0:1, 2:3]))
        wt(v, "v", (PV, i))
        _v.out = PV.inc(v.tensor_scalar(GTMP[0:1, 5:6], GTMP[0:1, 4:5], 1.0, -LAMBDA_INIT, ALU.mult, ALU.add))
    VS(_v); v_neg = _v.out

    def _t(t):
        wt(t, "t", (PV, v_neg))
        wt(t, "t", (LDS, ld_consts))
        wait_banks(t, ["PB0"])
        _t.out = PT.inc(t.matmul(PB[0][:, 0:1], ONESF[0:1, :], GTMP[0:1, 5:6], start=True, stop=True))
    TS(_t); t_bc = _t.out

    def _s(s):
        wt(s, "s", (PT, t_bc))
        _s.out = PS.inc(s.copy(NLAM[:], PB[0][:, 0:1]))
    SS(_s); lam_col = (PS, _s.out)
    bank_dep["PB0"] = lam_col

    strip_cons = [None, None]
    att_cons = [None] * 16
    p2_dep_all = ["P2a", "P2b", "P2c", "P2d"]
    y_done = None
    evac_flip = 0
    x1_done = None
    sq_cons = [0]
    qkv_war = []
    acc_state_jt = [None] * TT

    if emit_attn:
        strip_seq = [(m, h) for h in range(n_heads) for m in (wqT, wkT, wvT)]
        strip_val = {}
        next_strip = [0]

        def load_strip():
            i = next_strip[0]
            if i >= len(strip_seq):
                return
            mat, h = strip_seq[i]
            slot = i % 2

            def _g(g):
                wt(g, "g", strip_cons[slot])
                g.dma_start(out=WS[slot][:].rearrange("p (n m) -> p n m", n=DT),
                            in_=mat[:, h * HD:(h + 1) * HD].rearrange("(n p) m -> p n m", p=128)
                            ).then_inc(SWS[slot].h, 16)
                SWS[slot].n += 16
            GS(_g)
            strip_val[i] = SWS[slot].n
            next_strip[0] += 1

        load_strip()
        load_strip()

        for h in range(n_heads):
            for pj in range(3):
                si = h * 3 + pj
                slot = si % 2
                sval = strip_val[si]
                if pj < 2:
                    dstbuf = QT if pj == 0 else KT
                    for grp in range(8):
                        jt, tc = grp // 4, grp % 4
                        bn = grp % 2

                        def _t(t, jt=jt, tc=tc, bn=bn, slot=slot, sval=sval):
                            wt(t, "t", (LDG, ld_xt))
                            wt(t, "t", (SWS[slot], sval))
                            wait_banks(t, [BANKS[bn]])
                            for dt in range(DT):
                                mm = t.matmul(PB[bn][:],
                                              WS[slot][:, dt * HD + jt * 128: dt * HD + (jt + 1) * 128],
                                              XT[:, dt * T + tc * 512: dt * T + (tc + 1) * 512],
                                              start=(dt == 0), stop=(dt == DT - 1))
                            _t.out = PT.inc(mm)
                        TS(_t)
                        mmv = _t.out
                        dst = dstbuf[:, jt * T + tc * 512: jt * T + (tc + 1) * 512]
                        if evac_flip % 2 == 0:
                            def _s(s, dst=dst, bn=bn, mmv=mmv):
                                wt(s, "s", (PT, mmv))
                                for dep in qkv_war:
                                    wt(s, "s", dep)
                                _s.out = PS.inc(s.copy(dst, PB[bn][:]))
                            SS(_s)
                            rel_banks([BANKS[bn]], (PS, _s.out))
                        else:
                            def _v(v, dst=dst, bn=bn, mmv=mmv):
                                wt(v, "v", (PT, mmv))
                                for dep in qkv_war:
                                    wt(v, "v", dep)
                                _v.out = PV.inc(v.tensor_copy(dst, PB[bn][:]))
                            VS(_v)
                            rel_banks([BANKS[bn]], (PV, _v.out))
                        evac_flip += 1
                else:
                    for tt in range(TT):
                        bn = 2 + tt % 2

                        def _t(t, tt=tt, bn=bn, slot=slot, sval=sval):
                            wt(t, "t", (LDG, ld_xt))
                            wt(t, "t", (SWS[slot], sval))
                            wait_banks(t, [BANKS[bn]])
                            for dt in range(DT):
                                mm = t.matmul(PB[bn][:, 0:HD],
                                              XT[:, dt * T + tt * 128: dt * T + (tt + 1) * 128],
                                              WS[slot][:, dt * HD:(dt + 1) * HD],
                                              start=(dt == 0), stop=(dt == DT - 1))
                            _t.out = PT.inc(mm)
                        TS(_t)
                        mmv = _t.out
                        dst = VTOK[:, tt * HD:(tt + 1) * HD]
                        if evac_flip % 2 == 0:
                            def _s(s, dst=dst, bn=bn, mmv=mmv):
                                wt(s, "s", (PT, mmv))
                                for dep in qkv_war:
                                    wt(s, "s", dep)
                                _s.out = PS.inc(s.copy(dst, PB[bn][:, 0:HD]))
                            SS(_s)
                            rel_banks([BANKS[bn]], (PS, _s.out))
                        else:
                            def _v(v, dst=dst, bn=bn, mmv=mmv):
                                wt(v, "v", (PT, mmv))
                                for dep in qkv_war:
                                    wt(v, "v", dep)
                                _v.out = PV.inc(v.tensor_copy(dst, PB[bn][:, 0:HD]))
                            VS(_v)
                            rel_banks([BANKS[bn]], (PV, _v.out))
                        evac_flip += 1
                strip_cons[slot] = (PT, PT.n)
                load_strip()

            proj_done = [bank_dep[b] for b in BANKS[:4]]

            for m in range(2):
                def _vq(v, m=m):
                    for dep in proj_done:
                        wt(v, "v", dep)
                    wt(v, "v", (PT, sq_cons[0]))
                    for kt8 in range(8):
                        wt(v, "v", att_cons[kt8])
                    v.tensor_mul(SQ[:, 0:T], QT[:, m * T:(m + 1) * T], QT[:, m * T:(m + 1) * T])
                    _vq.out = PV.inc(v.tensor_mul(SQ[:, T:2 * T], KT[:, m * T:(m + 1) * T],
                                                  KT[:, m * T:(m + 1) * T]))
                VS(_vq)
                sq_val = _vq.out

                def _t(t, m=m, sq_val=sq_val):
                    wt(t, "t", (PV, sq_val))
                    wait_banks(t, p2_dep_all)
                    for c in range(4):
                        mm = t.matmul(P2[:, c * 512:(c + 1) * 512], ONESB[:],
                                      SQ[:, c * 512:(c + 1) * 512], start=True, stop=True)
                    _t.out = PT.inc(mm)
                TS(_t)
                ssq_val = _t.out

                def _s(s, m=m, ssq_val=ssq_val):
                    wt(s, "s", (PT, ssq_val))
                    i = PS.inc(s.activation(P2[:], P2[:], AF.Ln, scale=1.0 / HALF, bias=CST[:, 0:1]))
                    wt(s, "s", (PS, i))
                    _s.out = PS.inc(s.activation(P2[:], P2[:], AF.Exp, scale=-0.5, bias=CST[:, 2:3]))
                SS(_s)
                rq_val = _s.out

                def _v(v, m=m, rq_val=rq_val):
                    wt(v, "v", (PS, rq_val))
                    _v.out = PV.inc(v.tensor_tensor(QT[:, m * T:(m + 1) * T],
                                                    QT[:, m * T:(m + 1) * T], P2[:], ALU.mult))
                VS(_v)
                q1n_val = _v.out

                def _t(t, m=m, q1n_val=q1n_val):
                    wt(t, "t", (PV, q1n_val))
                    for c in range(4):
                        mm = t.matmul(P2[:, c * 512:(c + 1) * 512], ONESB[:],
                                      SQ[:, T + c * 512: T + (c + 1) * 512], start=True, stop=True)
                    _t.out = PT.inc(mm)
                TS(_t)
                ssqk_val = _t.out
                sq_cons[0] = ssqk_val

                def _s(s, m=m, ssqk_val=ssqk_val):
                    wt(s, "s", (PT, ssqk_val))
                    i = PS.inc(s.activation(P2[:], P2[:], AF.Ln, scale=1.0 / HALF, bias=CST[:, 0:1]))
                    wt(s, "s", (PS, i))
                    _s.out = PS.inc(s.activation(P2[:], P2[:], AF.Exp, scale=-0.5, bias=CST[:, 3:4]))
                SS(_s)
                rkb_val = _s.out

                def _v(v, m=m, rkb_val=rkb_val):
                    wt(v, "v", (PS, rkb_val))
                    _v.out = PV.inc(v.tensor_tensor(KT[:, m * T:(m + 1) * T],
                                                    KT[:, m * T:(m + 1) * T], P2[:], ALU.mult))
                VS(_v)
                k1n_val = _v.out
                rel_banks(p2_dep_all, (PV, k1n_val))

                if h == 0 and m == 1 and ("d_qt" in dbg or "d_kt" in dbg or "d_vt" in dbg):
                    def _g(g, k1n_val=k1n_val):
                        wt(g, "g", (PV, k1n_val))
                        if "d_qt" in dbg:
                            g.dma_start(out=dbg["d_qt"], in_=QT[:]).then_inc(DBG.h, 16); DBG.n += 16
                        if "d_kt" in dbg:
                            g.dma_start(out=dbg["d_kt"], in_=KT[:]).then_inc(DBG.h, 16); DBG.n += 16
                        if "d_vt" in dbg:
                            g.dma_start(out=dbg["d_vt"], in_=VTOK[:]).then_inc(DBG.h, 16); DBG.n += 16
                    GS(_g)

                for qb in range(NQB):
                    nkt = 4 * qb + 4
                    exp_done = [0] * nkt
                    for kt in range(nkt):
                        qs = max(0, kt * 128 - qb * 512)
                        bn = kt % 2

                        if "d_sc0" in dbg and h == 0 and m == 0 and qb == 3 and kt == 0:
                            def _sd0(s):
                                wt(s, "s", bank_dep["PB0"])
                                _sd0.out = PS.inc(s.copy(RB[:], PB[0][:]))
                            SS(_sd0)
                            def _gd0(g):
                                wt(g, "g", (PS, _sd0.out))
                                g.dma_start(out=dbg["d_sc0"], in_=RB[:]).then_inc(DBG.h, 16)
                                DBG.n += 16
                            GS(_gd0)
                            pre_dump_g = (DBG, DBG.n)
                            def _vw0(v, d=pre_dump_g):
                                wt(v, "v", d)
                            VS(_vw0)
                            bank_dep["PB0"] = (PS, _sd0.out)

                        def _t(t, kt=kt, qb=qb, qs=qs, bn=bn, m=m, k1n_val=k1n_val):
                            wt(t, "t", (PV, k1n_val))
                            wait_banks(t, [BANKS[bn]])
                            if "d_sc0" in dbg and h == 0 and m == 0 and qb == 3 and kt == 0:
                                wt(t, "t", (DBG, DBG.n))
                            _t.out = PT.inc(t.matmul(
                                PB[bn][:, qs:512],
                                KT[:, m * T + kt * 128: m * T + (kt + 1) * 128],
                                QT[:, m * T + qb * 512 + qs: m * T + (qb + 1) * 512],
                                start=True, stop=True))
                        TS(_t)
                        sc_val = _t.out

                        if "d_sc0" in dbg and h == 0 and m == 0 and qb == 3 and kt == 0:
                            pass
                        if "d_sc" in dbg and h == 0 and m == 0 and qb == 3 and kt == 0:
                            DSTG = al("DSTG", [128, 512], F32, offset=C_OFF + 36864)
                            def _vset(v):
                                _vset.out = PV.inc(v.memset(DSTG[:], -7.0))
                            VS(_vset)
                            def _sdump(s, bn=bn, sc_val=sc_val):
                                wt(s, "s", (PT, sc_val))
                                wt(s, "s", (PV, _vset.out))
                                _sdump.out = PS.inc(s.copy(DSTG[:], PB[bn][:]))
                            SS(_sdump)
                            def _gdump(g):
                                wt(g, "g", (PS, _sdump.out))
                                g.dma_start(out=dbg["d_sc"], in_=DSTG[:]).then_inc(DBG.h, 16)
                                DBG.n += 16
                            GS(_gdump)

                        mask_val = 0
                        if kt >= 4 * qb:
                            dio = kt * 128 - qb * 512

                            def _v(v, dio=dio, bn=bn, sc_val=sc_val):
                                wt(v, "v", (PT, sc_val))
                                _v.out = PV.inc(v.tensor_tensor(PB[bn][:, dio:dio + 128],
                                                                PB[bn][:, dio:dio + 128],
                                                                MASK[:], ALU.add))
                            VS(_v)
                            mask_val = _v.out

                        def _s(s, kt=kt, qs=qs, bn=bn, m=m, sc_val=sc_val, mask_val=mask_val):
                            wt(s, "s", (PT, sc_val))
                            if mask_val:
                                wt(s, "s", (PV, mask_val))
                            wt(s, "s", att_cons[kt])
                            if 8 <= kt < 12:
                                wt(s, "s", (PV, v_sums))
                            _s.out = PS.inc(s.activation(
                                ATT[:, kt * 512 + qs: (kt + 1) * 512],
                                PB[bn][:, qs:512], AF.Exp))
                        SS(_s)
                        exp_done[kt] = _s.out
                        rel_banks([BANKS[bn]], (PS, _s.out))

                    def _t(t, qb=qb, nkt=nkt, exp_done=exp_done):
                        wait_banks(t, ["P2a", "PB2", "PB3"])
                        for kt in range(nkt):
                            qs = max(0, kt * 128 - qb * 512)
                            wt(t, "t", (PS, exp_done[kt]))
                            t.matmul(P2[:, qs:512], ONESB[:],
                                     ATT[:, kt * 512 + qs: (kt + 1) * 512],
                                     start=(kt == 0), stop=(kt == nkt - 1))
                        for ct in range(2):
                            for kt in range(nkt):
                                qs = max(0, kt * 128 - qb * 512)
                                mm = t.matmul(PB[2 + ct][:, qs:512],
                                              VTOK[:, kt * HD + ct * 128: kt * HD + (ct + 1) * 128],
                                              ATT[:, kt * 512 + qs: (kt + 1) * 512],
                                              start=(kt == 0), stop=(kt == nkt - 1))
                        _t.out = PT.inc(mm)
                    TS(_t)
                    av_val = _t.out
                    for kt in range(nkt):
                        att_cons[kt] = (PT, av_val)

                    def _v(v, qb=qb, m=m, h=h, av_val=av_val):
                        wt(v, "v", (PT, av_val))
                        i = PV.inc(v.reciprocal(RB[:], P2[:, 0:512]))
                        wt(v, "v", (PV, i))
                        if m == 0:
                            for ct in range(2):
                                i = PV.inc(v.tensor_tensor(
                                    Y[:, (2 * h + ct) * T + qb * 512: (2 * h + ct) * T + (qb + 1) * 512],
                                    PB[2 + ct][:], RB[:], ALU.mult))
                        else:
                            wt(v, "v", lam_col)
                            for ct in range(2):
                                i0 = PV.inc(v.tensor_tensor(TMP[:], PB[2 + ct][:], RB[:], ALU.mult))
                                wt(v, "v", (PV, i0))
                                ysl = Y[:, (2 * h + ct) * T + qb * 512: (2 * h + ct) * T + (qb + 1) * 512]
                                i = PV.inc(v.scalar_tensor_tensor(ysl, TMP[:], NLAM[:, 0:1], ysl,
                                                                  ALU.mult, ALU.add))
                                wt(v, "v", (PV, i))
                        _v.out = i
                    VS(_v)
                    yv = _v.out
                    rel_banks(["PB2", "PB3", "P2a"], (PV, yv))

                    if h == 0 and m == 0 and qb == 3 and ("d_att" in dbg or "d_rb" in dbg):
                        def _g(g, yv=yv):
                            wt(g, "g", (PV, yv))
                            if "d_att" in dbg:
                                g.dma_start(out=dbg["d_att"], in_=ATT[:]).then_inc(DBG.h, 16)
                                DBG.n += 16
                            if "d_rb" in dbg:
                                g.dma_start(out=dbg["d_rb"], in_=RB[:]).then_inc(DBG.h, 16)
                                DBG.n += 16
                        GS(_g)
                        for _kt in range(16):
                            att_cons[_kt] = (DBG, DBG.n)

            qkv_war.clear()
            qkv_war.extend([(PT, PT.n), (PS, PS.n), (PV, PV.n)])

        y_done = (PV, PV.n)
        dump_guard = [None]
        if "d_y" in dbg:
            def _g(g):
                wt(g, "g", y_done)
                for ct in range(DT):
                    g.dma_start(out=dbg["d_y"][ct * 128:(ct + 1) * 128, :],
                                in_=Y[:, ct * T:(ct + 1) * T]).then_inc(DBG.h, 16)
                    DBG.n += 16
            GS(_g)
            dump_guard[0] = (DBG, DBG.n)

        res1_done = [None] * TT
        oxf_cons = [None, None]
        for jb in range(8):
            slot = jb % 2

            def _g(g, jb=jb, slot=slot):
                wt(g, "g", strip_cons[slot])
                g.dma_start(out=WS[slot][:].rearrange("p (n m) -> p n m", n=DT),
                            in_=woT[:, jb * HD:(jb + 1) * HD].rearrange("(n p) m -> p n m", p=128)
                            ).then_inc(SWS[slot].h, 16)
                SWS[slot].n += 16
            GS(_g)
            wval = SWS[slot].n
            for ji in range(2):
                jt = jb * 2 + ji
                xslot = jt % 2

                def _y(y, jt=jt, xslot=xslot):
                    wt(y, "y", y_done)
                    wt(y, "y", oxf_cons[xslot])
                    y.dma_start(out=OXF[xslot][:], in_=xt[jt * 128:(jt + 1) * 128, :]
                                ).then_inc(SOX[xslot].h, 16)
                    SOX[xslot].n += 16
                YS(_y)
                xval = SOX[xslot].n
                last = None
                for tc in range(4):
                    bn = tc % 2

                    def _t(t, ji=ji, tc=tc, bn=bn, slot=slot, wval=wval):
                        wt(t, "t", (SWS[slot], wval))
                        wt(t, "t", y_done)
                        wait_banks(t, [BANKS[bn]])
                        for ct in range(DT):
                            mm = t.matmul(PB[bn][:],
                                          WS[slot][:, ct * HD + ji * 128: ct * HD + (ji + 1) * 128],
                                          Y[:, ct * T + tc * 512: ct * T + (tc + 1) * 512],
                                          start=(ct == 0), stop=(ct == DT - 1))
                        _t.out = PT.inc(mm)
                    TS(_t)
                    mmv = _t.out

                    def _v(v, jt=jt, tc=tc, bn=bn, xslot=xslot, xval=xval, mmv=mmv):
                        wt(v, "v", (PT, mmv))
                        wt(v, "v", (SOX[xslot], xval))
                        _v.out = PV.inc(v.scalar_tensor_tensor(
                            RES1[:, jt * T + tc * 512: jt * T + (tc + 1) * 512],
                            PB[bn][:], 1.0 - LAMBDA_INIT,
                            OXF[xslot][:, tc * 512:(tc + 1) * 512],
                            ALU.mult, ALU.add))
                    VS(_v)
                    last = (PV, _v.out)
                    rel_banks([BANKS[bn]], last)
                oxf_cons[xslot] = last
                res1_done[jt] = last
            strip_cons[slot] = (PT, PT.n)

        if "d_res1" in dbg:
            def _g(g):
                for jt in range(TT):
                    wt(g, "g", res1_done[jt])
                    g.dma_start(out=dbg["d_res1"][jt * 128:(jt + 1) * 128, :],
                                in_=RES1[:, jt * T:(jt + 1) * T]).then_inc(DBG.h, 16)
                    DBG.n += 16
            GS(_g)

        sqd = [None, None]
        stats_val = None
        for jt in range(TT):
            slot = jt % 2

            def _vv(v, jt=jt, slot=slot):
                wt(v, "v", res1_done[jt])
                wt(v, "v", sqd[slot])
                _vv.out = PV.inc(v.tensor_mul(LNSQ[slot][:], RES1[:, jt * T:(jt + 1) * T],
                                              RES1[:, jt * T:(jt + 1) * T]))
            VS(_vv)
            sqv = _vv.out

            def _t(t, jt=jt, slot=slot, sqv=sqv):
                if jt == 0:
                    wait_banks(t, BANKS[:4] + p2_dep_all)
                wt(t, "t", res1_done[jt])
                wt(t, "t", (PV, sqv))
                for c in range(4):
                    t.matmul(PB[c][:], ONESB[:],
                             RES1[:, jt * T + c * 512: jt * T + (c + 1) * 512],
                             start=(jt == 0), stop=(jt == TT - 1))
                for c in range(4):
                    mm = t.matmul(P2[:, c * 512:(c + 1) * 512], ONESB[:],
                                  LNSQ[slot][:, c * 512:(c + 1) * 512],
                                  start=(jt == 0), stop=(jt == TT - 1))
                _t.out = PT.inc(mm)
            TS(_t)
            sqd[slot] = (PT, _t.out)
            stats_val = _t.out

        mr_val = None
        for c in range(4):
            def _v(v, c=c):
                wt(v, "v", (PT, stats_val))
                i = PV.inc(v.tensor_scalar(LNM[:, c * 512:(c + 1) * 512], PB[c][:],
                                           1.0 / T, None, ALU.mult))
                wt(v, "v", (PV, i))
                v.tensor_scalar(P2[:, c * 512:(c + 1) * 512], P2[:, c * 512:(c + 1) * 512],
                                1.0 / T, None, ALU.mult)
                i = PV.inc(v.tensor_tensor(LNT[:], LNM[:, c * 512:(c + 1) * 512],
                                           LNM[:, c * 512:(c + 1) * 512], ALU.mult))
                wt(v, "v", (PV, i))
                _v.out = PV.inc(v.tensor_tensor(P2[:, c * 512:(c + 1) * 512],
                                                P2[:, c * 512:(c + 1) * 512], LNT[:], ALU.subtract))
            VS(_v)
            var_val = _v.out

            def _s(s, c=c, var_val=var_val):
                wt(s, "s", (PV, var_val))
                i = PS.inc(s.activation(P2[:, c * 512:(c + 1) * 512], P2[:, c * 512:(c + 1) * 512],
                                        AF.Ln, bias=CST[:, 1:2]))
                wt(s, "s", (PS, i))
                _s.out = PS.inc(s.activation(P2[:, c * 512:(c + 1) * 512],
                                             P2[:, c * 512:(c + 1) * 512], AF.Exp, scale=-0.5, bias=CST[:, 3:4]))
            SS(_s)
            rstd_val = _s.out

            def _v(v, c=c, rstd_val=rstd_val):
                wt(v, "v", (PS, rstd_val))
                _v.out = PV.inc(v.tensor_tensor(LNM[:, c * 512:(c + 1) * 512],
                                                LNM[:, c * 512:(c + 1) * 512],
                                                P2[:, c * 512:(c + 1) * 512], ALU.mult))
            VS(_v)
            mr_val = _v.out

        for jt in range(TT):
            def _v(v, jt=jt, mr_val=mr_val):
                wt(v, "v", (PV, mr_val))
                if dump_guard[0] is not None:
                    wt(v, "v", dump_guard[0])
                for c in range(4):
                    i0 = PV.inc(v.tensor_tensor(TMP[:],
                                                RES1[:, jt * T + c * 512: jt * T + (c + 1) * 512],
                                                P2[:, c * 512:(c + 1) * 512], ALU.mult))
                    wt(v, "v", (PV, i0))
                    i = PV.inc(v.tensor_tensor(X1T[:, jt * T + c * 512: jt * T + (c + 1) * 512],
                                               TMP[:], LNM[:, c * 512:(c + 1) * 512], ALU.subtract))
                    wt(v, "v", (PV, i))
                _v.out = i
            VS(_v)
            x1_done = (PV, _v.out)
        rel_banks(BANKS[:4] + p2_dep_all, x1_done)
    else:
        for jt in range(TT):
            def _v(v, jt=jt):
                wt(v, "v", (LDG, ld_xt))
                _v.out = PV.inc(v.tensor_copy(X1T[:, jt * T:(jt + 1) * T], XT[:, jt * T:(jt + 1) * T]))
            VS(_v)
            x1_done = (PV, _v.out)

    if "d_x1" in dbg:
        def _g(g):
            wt(g, "g", x1_done)
            for jt in range(TT):
                g.dma_start(out=dbg["d_x1"][jt * 128:(jt + 1) * 128, :],
                            in_=X1T[:, jt * T:(jt + 1) * T]).then_inc(DBG.h, 16)
                DBG.n += 16
        GS(_g)

    if not emit_ffn:
        def _g(g):
            wt(g, "g", x1_done)
            for jt in range(TT):
                g.dma_start(out=outT[jt * 128:(jt + 1) * 128, :],
                            in_=X1T[:, jt * T:(jt + 1) * T]).then_inc(DBG.h, 16)
                DBG.n += 16
            g.wait_ge(DBG.h, DBG.n)
        GS(_g)
        ctx.close()
        return nc, dbg

    def _t(t):
        wt(t, "t", x1_done)
        wt(t, "t", (LDG, ld_xt))
        wait_banks(t, p2_dep_all)
        for dt in range(DT):
            for c in range(4):
                mm = t.matmul(P2[0:8, c * 512:(c + 1) * 512],
                              GKB[:, dt * E:(dt + 1) * E],
                              X1T[:, dt * T + c * 512: dt * T + (c + 1) * 512],
                              start=(dt == 0), stop=(dt == DT - 1))
        _t.out = PT.inc(mm)
    TS(_t)
    glog_val = _t.out

    def _s(s):
        wt(s, "s", (PT, glog_val))
        _s.out = PS.inc(s.copy(LOGIT[:], P2[0:8, :]))
    SS(_s)
    logit_val = _s.out
    rel_banks(p2_dep_all, (PS, logit_val))

    cw_val = None
    gtok_val = 0
    cwev_val = 0
    for tt in range(TT):
        def _t(t, tt=tt):
            wt(t, "t", (PS, logit_val))
            if tt == 0:
                wait_banks(t, ["PB0"])
            else:
                wt(t, "t", (PV, gtok_val))
            _t.out = PT.inc(t.transpose(PB[0][:, 0:8], LOGIT[0:8, tt * 128:(tt + 1) * 128],
                                        IDF[0:8, 0:8]))
        TS(_t)
        gtr_val = _t.out

        def _v(v, tt=tt, gtr_val=gtr_val):
            wt(v, "v", (PT, gtr_val))
            _v.out = PV.inc(v.tensor_copy(CWTOK[:, tt * E:(tt + 1) * E], PB[0][:, 0:8]))
        VS(_v)
        gtok_val = _v.out

    for tt in range(TT):
        lg = CWTOK[:, tt * E:(tt + 1) * E]

        def _v(v, lg=lg):
            wt(v, "v", (PV, gtok_val))
            i = PV.inc(v.reduce_max(GTMP[:, 0:1], lg, axis=AX.X))
            wt(v, "v", (PV, i))
            i = PV.inc(v.tensor_scalar(GEQ[0][:], lg, GTMP[:, 0:1], None, ALU.is_equal))
            wt(v, "v", (PV, i))
            i = PV.inc(v.scalar_tensor_tensor(GEQ[1][:], GEQ[0][:], -1e30, lg, ALU.mult, ALU.add))
            wt(v, "v", (PV, i))
            i = PV.inc(v.reduce_max(GTMP[:, 1:2], GEQ[1][:], axis=AX.X))
            wt(v, "v", (PV, i))
            _v.out = PV.inc(v.tensor_sub(GTMP[:, 2:3], GTMP[:, 1:2], GTMP[:, 0:1]))
        VS(_v)
        g1_val = _v.out

        def _s(s, g1_val=g1_val):
            wt(s, "s", (PV, g1_val))
            _s.out = PS.inc(s.activation(GTMP[:, 3:4], GTMP[:, 2:3], AF.Exp))
        SS(_s)
        g2_val = _s.out

        def _v(v, lg=lg, g2_val=g2_val):
            wt(v, "v", (PS, g2_val))
            i = PV.inc(v.tensor_scalar(GTMP[:, 4:5], GTMP[:, 3:4], 1.0, 1.0, ALU.mult, ALU.add))
            wt(v, "v", (PV, i))
            i = PV.inc(v.reciprocal(GTMP[:, 5:6], GTMP[:, 4:5]))
            wt(v, "v", (PV, i))
            i = PV.inc(v.tensor_tensor(GTMP[:, 6:7], GTMP[:, 3:4], GTMP[:, 5:6], ALU.mult))
            wt(v, "v", (PV, i))
            i = PV.inc(v.tensor_scalar(GTMP[:, 7:8], GTMP[:, 6:7], -1.0, 1.0, ALU.mult, ALU.add))
            wt(v, "v", (PV, i))
            i = PV.inc(v.tensor_scalar(GEQ[2][:], GEQ[1][:], GTMP[:, 1:2], None, ALU.is_equal))
            wt(v, "v", (PV, i))
            i = PV.inc(v.tensor_scalar(GEQ[0][:], GEQ[0][:], GTMP[:, 7:8], None, ALU.mult))
            wt(v, "v", (PV, i))
            _v.out = PV.inc(v.scalar_tensor_tensor(lg, GEQ[2][:], GTMP[:, 6:7], GEQ[0][:],
                                                   ALU.mult, ALU.add))
        VS(_v)
        cw_val = _v.out

        def _t(t, tt=tt, lg=lg, cw_val=cw_val):
            wt(t, "t", (PV, cw_val))
            if tt == 0:
                wait_banks(t, ["PB1"])
            else:
                wt(t, "t", (PS, cwev_val))
            _t.out = PT.inc(t.transpose(PB[1][0:8, 0:128], lg, IDF[:]))
        TS(_t)
        cwtr_val = _t.out

        def _s(s, tt=tt, cwtr_val=cwtr_val):
            wt(s, "s", (PT, cwtr_val))
            _s.out = PS.inc(s.copy(CWT[:, tt * 128:(tt + 1) * 128], PB[1][0:8, 0:128]))
        SS(_s)
        cwev_val = _s.out
    cw_done = (PS, cwev_val)
    rel_banks(["PB0"], (PV, gtok_val))
    rel_banks(["PB1"], cw_done)

    if "d_cw" in dbg:
        def _y(y):
            wt(y, "y", cw_done)
            y.dma_start(out=dbg["d_cw"], in_=CWT[:]).then_inc(DBG.h, 16)
            DBG.n += 16
        YS(_y)

    gate_guard = (PT, PT.n)
    h0p = [[("PB0", PB[0][:]), ("PB1", PB[1][:])],
           [("P2a", P2[:, 0:512]), ("P2b", P2[:, 512:1024])]]
    h1p = [[("PB2", PB[2][:]), ("PB3", PB[3][:])],
           [("P2c", P2[:, 1024:1536]), ("P2d", P2[:, 1536:2048])]]
    wi_cons = [[gate_guard, gate_guard], [gate_guard, gate_guard]]
    wo_cons = [gate_guard, gate_guard]
    h_cons = None
    acc_read = [None] * TT
    outs_cons = [None, None]
    silu_cons = None
    ft2_cons = None
    cwb_cons = None

    half_guard = []
    for hf in range(2):
        toff = hf * TCH
        for e in range(E):
            def _t(t, e=e, hf=hf):
                wt(t, "t", cw_done)
                for dep in half_guard:
                    wt(t, "t", dep)
                wait_banks(t, ["P2a", "P2b"])
                for c in range(2):
                    mm = t.matmul(P2[:, c * 512:(c + 1) * 512], ESEL[:, e * 128:(e + 1) * 128],
                                  CWT[:, hf * TCH + c * 512: hf * TCH + (c + 1) * 512],
                                  start=True, stop=True)
                _t.out = PT.inc(mm)
            TS(_t)
            cwbmm_val = _t.out

            def _v(v, cwbmm_val=cwbmm_val):
                wt(v, "v", (PT, cwbmm_val))
                wt(v, "v", cwb_cons)
                for dep in half_guard:
                    wt(v, "v", dep)
                _v.out = PV.inc(v.tensor_copy(CWB[:], P2[:, 0:1024]))
            VS(_v)
            cwb_val = _v.out
            rel_banks(["P2a", "P2b"], (PV, cwb_val))

            hev_val = None
            for mt in range(TT):
                par = mt % 2

                def _g(g, e=e, mt=mt, par=par):
                    for wsel, bufs, mat, sem in ((0, WI0S, wi0, SWI0[par]), (1, WI1S, wi1, SWI1[par])):
                        wt(g, "g", wi_cons[wsel][par])
                        g.dma_start(out=bufs[par][:].rearrange("p (n m) -> p n m", n=DT),
                                    in_=mat[e, :, mt * 128:(mt + 1) * 128]
                                    .rearrange("(n p) m -> p n m", p=128)).then_inc(sem.h, 16)
                        sem.n += 16
                GS(_g)
                wival = (SWI0[par].n, SWI1[par].n)

                def _t(t, mt=mt, par=par, wival=wival, toff=toff):
                    wt(t, "t", (SWI0[par], wival[0]))
                    wt(t, "t", (SWI1[par], wival[1]))
                    wt(t, "t", x1_done)
                    wait_banks(t, [h0p[par][0][0], h0p[par][1][0], h1p[par][0][0], h1p[par][1][0]])
                    for dt in range(DT):
                        for c in range(2):
                            t.matmul(h0p[par][c][1],
                                     WI0S[par][:, dt * 128:(dt + 1) * 128],
                                     X1T[:, dt * T + toff + c * 512: dt * T + toff + (c + 1) * 512],
                                     start=(dt == 0), stop=(dt == DT - 1))
                    for dt in range(DT):
                        for c in range(2):
                            mm = t.matmul(h1p[par][c][1],
                                          WI1S[par][:, dt * 128:(dt + 1) * 128],
                                          X1T[:, dt * T + toff + c * 512: dt * T + toff + (c + 1) * 512],
                                          start=(dt == 0), stop=(dt == DT - 1))
                    _t.out = PT.inc(mm)
                TS(_t)
                hmm_val = _t.out
                wi_cons[0][par] = (PT, hmm_val)
                wi_cons[1][par] = (PT, hmm_val)

                def _s(s, par=par, hmm_val=hmm_val):
                    wt(s, "s", (PT, hmm_val))
                    wt(s, "s", silu_cons)
                    for dep in half_guard:
                        wt(s, "s", dep)
                    for c in range(2):
                        i = PS.inc(s.activation(SILU[:, c * 512:(c + 1) * 512],
                                                h0p[par][c][1], AF.Silu))
                    _s.out = i
                SS(_s)
                silu_val = _s.out

                def _v(v, mt=mt, par=par, hmm_val=hmm_val, silu_val=silu_val, cwb_val=cwb_val):
                    wt(v, "v", (PS, silu_val))
                    wt(v, "v", (PT, hmm_val))
                    wt(v, "v", (PV, cwb_val))
                    wt(v, "v", h_cons)
                    wt(v, "v", ft2_cons)
                    for c in range(2):
                        i0 = PV.inc(v.tensor_tensor(FT2[:, c * 512:(c + 1) * 512],
                                                    SILU[:, c * 512:(c + 1) * 512],
                                                    h1p[par][c][1], ALU.mult))
                    wt(v, "v", (PV, i0))
                    _v.out = PV.inc(v.tensor_tensor(H[:, mt * TCH:(mt + 1) * TCH],
                                                    FT2[:], CWB[:], ALU.mult))
                VS(_v)
                hev_val = _v.out
                silu_cons = (PV, hev_val)
                ft2_cons = (PV, hev_val)
                dep = (PV, hev_val)
                rel_banks([h0p[par][0][0], h0p[par][1][0], h1p[par][0][0], h1p[par][1][0]], dep)
            cwb_cons = (PV, hev_val)
            all_h = hev_val

            for jt in range(TT):
                slot = jt % 2

                def _g(g, e=e, jt=jt, slot=slot):
                    wt(g, "g", wo_cons[slot])
                    g.dma_start(out=WOS[slot][:].rearrange("p (n m) -> p n m", n=DT),
                                in_=wom[e, :, jt * 128:(jt + 1) * 128]
                                .rearrange("(n p) m -> p n m", p=128)).then_inc(SWO[slot].h, 16)
                    SWO[slot].n += 16
                GS(_g)
                woval = SWO[slot].n
                for c in range(2):
                    bn = c

                    def _t(t, c=c, bn=bn, slot=slot, woval=woval, all_h=all_h):
                        wt(t, "t", (SWO[slot], woval))
                        wt(t, "t", (PV, all_h))
                        wait_banks(t, [BANKS[bn]])
                        for mt in range(TT):
                            mm = t.matmul(PB[bn][:],
                                          WOS[slot][:, mt * 128:(mt + 1) * 128],
                                          H[:, mt * TCH + c * 512: mt * TCH + (c + 1) * 512],
                                          start=(mt == 0), stop=(mt == TT - 1))
                        _t.out = PT.inc(mm)
                    TS(_t)
                    omm_val = _t.out

                    def _v(v, e=e, jt=jt, c=c, bn=bn, omm_val=omm_val):
                        wt(v, "v", (PT, omm_val))
                        dst = ACC[:, jt * TCH + c * 512: jt * TCH + (c + 1) * 512]
                        if e == 0:
                            wt(v, "v", acc_read[jt])
                            i = PV.inc(v.tensor_copy(dst, PB[bn][:]))
                        else:
                            i = PV.inc(v.tensor_tensor(dst, dst, PB[bn][:], ALU.add))
                        _v.out = i
                    VS(_v)
                    rel_banks([BANKS[bn]], (PV, _v.out))
                acc_state_jt[jt] = (PV, _v.out)
                wo_cons[slot] = (PT, PT.n)
            h_cons = (PT, PT.n)

        r2 = [None] * TT
        for jt in range(TT):
            def _v(v, jt=jt, toff=toff):
                wt(v, "v", acc_state_jt[jt])
                _v.out = PV.inc(v.tensor_tensor(ACC[:, jt * TCH:(jt + 1) * TCH],
                                                ACC[:, jt * TCH:(jt + 1) * TCH],
                                                X1T[:, jt * T + toff: jt * T + toff + TCH], ALU.add))
            VS(_v)
            r2[jt] = (PV, _v.out)

        sqd2 = [None, None]
        stats2_val = None
        for jt in range(TT):
            slot = jt % 2

            def _s(s, jt=jt, slot=slot):
                wt(s, "s", r2[jt])
                wt(s, "s", sqd2[slot])
                wt(s, "s", h_cons)
                _s.out = PS.inc(s.activation(SQ2[:, slot * TCH:(slot + 1) * TCH],
                                             ACC[:, jt * TCH:(jt + 1) * TCH], AF.Square))
            SS(_s)
            sq2_val = _s.out

            def _t(t, jt=jt, slot=slot, sq2_val=sq2_val):
                if jt == 0:
                    wait_banks(t, p2_dep_all)
                wt(t, "t", r2[jt])
                wt(t, "t", (PS, sq2_val))
                for c in range(2):
                    t.matmul(P2[:, c * 512:(c + 1) * 512], ONESF[:],
                             ACC[:, jt * TCH + c * 512: jt * TCH + (c + 1) * 512],
                             start=(jt == 0), stop=(jt == TT - 1))
                for c in range(2):
                    mm = t.matmul(P2[:, 1024 + c * 512: 1024 + (c + 1) * 512], ONESB[:],
                                  SQ2[:, slot * TCH + c * 512: slot * TCH + (c + 1) * 512],
                                  start=(jt == 0), stop=(jt == TT - 1))
                _t.out = PT.inc(mm)
            TS(_t)
            sqd2[slot] = (PT, _t.out)
            stats2_val = _t.out

        def _v(v, stats2_val=stats2_val):
            wt(v, "v", (PT, stats2_val))
            i = PV.inc(v.tensor_scalar(LNM2[:], P2[:, 0:1024], 1.0 / T, None, ALU.mult))
            wt(v, "v", (PV, i))
            v.tensor_scalar(P2[:, 1024:2048], P2[:, 1024:2048], 1.0 / T, None, ALU.mult)
            i = PV.inc(v.tensor_tensor(CWB[:], LNM2[:], LNM2[:], ALU.mult))
            wt(v, "v", (PV, i))
            _v.out = PV.inc(v.tensor_tensor(P2[:, 1024:2048], P2[:, 1024:2048],
                                            CWB[:], ALU.subtract))
        VS(_v)
        var2_val = _v.out

        def _s(s, var2_val=var2_val):
            wt(s, "s", (PV, var2_val))
            i = PS.inc(s.activation(P2[:, 1024:2048], P2[:, 1024:2048], AF.Ln, bias=CST[:, 1:2]))
            wt(s, "s", (PS, i))
            _s.out = PS.inc(s.activation(P2[:, 1024:2048], P2[:, 1024:2048], AF.Exp, scale=-0.5, bias=CST[:, 3:4]))
        SS(_s)
        rstd2_val = _s.out

        def _v(v, rstd2_val=rstd2_val):
            wt(v, "v", (PS, rstd2_val))
            _v.out = PV.inc(v.tensor_tensor(LNM2[:], LNM2[:], P2[:, 1024:2048], ALU.mult))
        VS(_v)
        mr2_val = _v.out

        for jt in range(TT):
            slot = jt % 2

            def _v(v, jt=jt, slot=slot, mr2_val=mr2_val):
                wt(v, "v", (PV, mr2_val))
                wt(v, "v", h_cons)
                dep = outs_cons[slot]
                if dep is not None:
                    wt(v, "v", dep)
                i0 = PV.inc(v.tensor_tensor(OUTS[slot][:], ACC[:, jt * TCH:(jt + 1) * TCH],
                                            P2[:, 1024:2048], ALU.mult))
                wt(v, "v", (PV, i0))
                _v.out = PV.inc(v.tensor_tensor(OUTS[slot][:], OUTS[slot][:], LNM2[:], ALU.subtract))
            VS(_v)
            o2_val = _v.out
            acc_read[jt] = (PV, o2_val)

            def _y(y, jt=jt, slot=slot, toff=toff, o2_val=o2_val):
                wt(y, "y", (PV, o2_val))
                y.dma_start(out=outT[jt * 128:(jt + 1) * 128, toff:toff + TCH],
                            in_=OUTS[slot][:]).then_inc(SST[slot].h, 16)
                SST[slot].n += 16
            YS(_y)
            outs_cons[slot] = (SST[slot], SST[slot].n)
        rel_banks(p2_dep_all, (PV, mr2_val))
        rel_banks(["P2c", "P2d"], acc_read[TT - 1])
        half_guard = [(PT, PT.n), (PV, PV.n), (PS, PS.n), (SST[0], SST[0].n), (SST[1], SST[1].n)]

    def _y(y):
        y.wait_ge(SST[0].h, SST[0].n)
        y.wait_ge(SST[1].h, SST[1].n)
    YS(_y)
    ctx.close()
    import concourse.mybir as _mb
    for _alloc in nc.main_func.allocations:
        if isinstance(_alloc, _mb.MemoryLocationSet) and _alloc.memorylocations:
            for _m in _alloc.memorylocations:
                if _m.name.startswith("const-"):
                    assert _m.addr + 64 <= A_OFF, (_m.name, _m.addr)
    return nc, dbg


def _prep_inputs(inputs):
    f32 = np.float32
    x = np.asarray(inputs["L0_attn_x"], dtype=f32)
    base = {
        "wqT": np.ascontiguousarray(np.asarray(inputs["L0_attn_wq"], f32).T),
        "wkT": np.ascontiguousarray(np.asarray(inputs["L0_attn_wk"], f32).T),
        "wvT": np.ascontiguousarray(np.asarray(inputs["L0_attn_wv"], f32).T),
        "woT": np.ascontiguousarray(np.asarray(inputs["L0_attn_wo"], f32).T),
        "gk": np.ascontiguousarray(np.asarray(inputs["L0_ffn_gate_kernel"], f32)),
        "lam": np.concatenate([np.asarray(inputs[k], f32).reshape(1, -1) for k in
                               ("L0_attn_lq1", "L0_attn_lk1", "L0_attn_lq2", "L0_attn_lk2")],
                              axis=1),
        "wi0": np.ascontiguousarray(np.asarray(inputs["L0_ffn_wi_0"], f32)),
        "wi1": np.ascontiguousarray(np.asarray(inputs["L0_ffn_wi_1"], f32)),
        "wom": np.ascontiguousarray(np.asarray(inputs["L0_ffn_wo"], f32)),
        "idf": np.eye(128, dtype=f32),
        "onesb_in": np.ones((128, 128), f32),
        "onesf_in": np.ones((128, 128), f32),
        "maskf": np.where(np.arange(128)[:, None] > np.arange(128)[None, :],
                          f32(-1e30), f32(0.0)).astype(f32),
    }
    cst = np.zeros((128, 8), f32)
    cst[:, 0] = RMS_EPS
    cst[:, 1] = LN_EPS
    cst[:, 2] = -0.5 * np.log(128.0)
    base["cst_in"] = cst
    esel = np.zeros((8, E * 128), f32)
    for e in range(E):
        esel[e, e * 128:(e + 1) * 128] = 1.0
    base["esel_in"] = esel
    in_maps = []
    for b in range(8):
        m = dict(base)
        m["xt"] = np.ascontiguousarray(x[b].T)
        in_maps.append(m)
    return in_maps


def kernel(**inputs):
    in_maps = _prep_inputs(inputs)
    nc, _ = build()
    res = run_bass_kernel_spmd(nc, in_maps, core_ids=list(range(8)))
    out = np.empty((8, T, D), dtype=np.float32)
    for b in range(8):
        out[b] = np.asarray(res.results[b]["outT"], dtype=np.float32).T
    return out


# version 02 (final)
# speedup vs baseline: 1.0230x; 1.0230x over previous
import numpy as np
import ml_dtypes
import concourse.bass as bass
import concourse.mybir as mybir
from concourse.bass_utils import run_bass_kernel_spmd

F32 = mybir.dt.float32
BF16 = mybir.dt.bfloat16
AF = mybir.ActivationFunctionType
ALU = mybir.AluOpType
AX = mybir.AxisListType

D = 2048
T = 2048
NH = 8
HD = 256
HALF = 128
DT = D // 128
TT = T // 128
NQB = T // 512
E = 8
TCH = 1024
RMS_EPS = 1.1920929e-07
LN_EPS = 1e-5
LAMBDA_INIT = 0.8

A_OFF = 17408
B_OFF = 82944
C_OFF = 148480
D_OFF = 214656


class Ctr:
    def __init__(self, handle, name):
        self.h = handle
        self.name = name
        self.n = 0

    def inc(self, instr, k=1):
        instr.then_inc(self.h, k)
        self.n += k
        return self.n


def build(n_heads=NH, emit_attn=True, emit_ffn=True, debug=()):
    nc = bass.Bass("TRN2", target_bir_lowering=False, debug=False)

    xt = nc.dram_tensor("xt", [D, T], F32, kind="ExternalInput").ap()
    wqT = nc.dram_tensor("wqT", [D, D], F32, kind="ExternalInput").ap()
    wkT = nc.dram_tensor("wkT", [D, D], F32, kind="ExternalInput").ap()
    wvT = nc.dram_tensor("wvT", [D, D], F32, kind="ExternalInput").ap()
    woT = nc.dram_tensor("woT", [D, D], F32, kind="ExternalInput").ap()
    gk = nc.dram_tensor("gk", [D, E], F32, kind="ExternalInput").ap()
    lam = nc.dram_tensor("lam", [1, 4 * HD], F32, kind="ExternalInput").ap()
    wi0 = nc.dram_tensor("wi0", [E, D, D], F32, kind="ExternalInput").ap()
    wi1 = nc.dram_tensor("wi1", [E, D, D], F32, kind="ExternalInput").ap()
    wom = nc.dram_tensor("wom", [E, D, D], F32, kind="ExternalInput").ap()
    idf = nc.dram_tensor("idf", [128, 128], F32, kind="ExternalInput").ap()
    onesb_in = nc.dram_tensor("onesb_in", [128, 128], F32, kind="ExternalInput").ap()
    onesf_in = nc.dram_tensor("onesf_in", [128, 128], F32, kind="ExternalInput").ap()
    maskf = nc.dram_tensor("maskf", [128, 128], F32, kind="ExternalInput").ap()
    cst_in = nc.dram_tensor("cst_in", [128, 8], F32, kind="ExternalInput").ap()
    esel_in = nc.dram_tensor("esel_in", [8, E * 128], F32, kind="ExternalInput").ap()
    outT = nc.dram_tensor("outT", [D, T], F32, kind="ExternalOutput").ap()
    dbg = {}
    _dbg_shapes = {"d_cw": [E, T], "d_qt": [128, 2 * T], "d_kt": [128, 2 * T],
                   "d_vt": [128, TT * HD], "d_att": [128, 16 * 512], "d_rb": [128, 512],
                   "d_sc": [128, 512], "d_sc0": [128, 512]}
    for name in debug:
        shape = _dbg_shapes.get(name, [D, T])
        dbg[name] = nc.dram_tensor(name, shape, F32, kind="ExternalOutput").ap()

    al = nc.alloc_sbuf_tensor_at
    XT = al("XT", [128, DT * T], BF16, offset=A_OFF)
    RES1 = al("RES1", [128, DT * T], BF16, offset=A_OFF)
    ACC = al("ACC", [128, DT * TCH], F32, offset=A_OFF)
    Y = al("Y", [128, DT * T], BF16, offset=B_OFF)
    X1T = al("X1T", [128, DT * T], BF16, offset=B_OFF)
    QT = al("QT", [128, 2 * T], BF16, offset=C_OFF + 0)
    KT = al("KT", [128, 2 * T], BF16, offset=C_OFF + 8192)
    VTOK = al("VTOK", [128, TT * HD], BF16, offset=C_OFF + 16384)
    ATT = al("ATT", [128, 16 * 512], BF16, offset=C_OFF + 24576)
    SQ = al("SQ", [128, T], BF16, offset=C_OFF + 61440)
    LAMV = al("LAMV", [1, 4 * HD], F32, offset=C_OFF + 32768)
    WS = [al(f"WS{i}", [128, DT * HD], BF16, offset=C_OFF + 40960 + i * 8192) for i in range(2)]
    TMP = al("TMPB", [128, 512], BF16, offset=C_OFF + 57344)
    RB = al("RB", [128, 512], F32, offset=C_OFF + 58368)
    OXF = [al(f"OXF{i}", [128, T], F32, offset=C_OFF + 0 + i * 8192) for i in range(2)]
    LNSQ = [al(f"LNSQ{i}", [128, T], BF16, offset=C_OFF + 16384 + i * 4096) for i in range(2)]
    LNM = al("LNM", [128, T], F32, offset=C_OFF + 24576)
    LNT = al("LNT", [128, 512], F32, offset=C_OFF + 61440)
    LOGIT = al("LOGIT", [8, T], F32, offset=C_OFF + 24576)
    CWTOK = al("CWTOK", [128, TT * E], F32, offset=C_OFF + 60416)
    H = al("H", [128, DT * TCH], BF16, offset=C_OFF + 0)
    WI0S = [al(f"WI0S{i}", [128, DT * 128], BF16, offset=C_OFF + 32768 + i * 4096) for i in range(2)]
    WI1S = [al(f"WI1S{i}", [128, DT * 128], BF16, offset=C_OFF + 40960 + i * 4096) for i in range(2)]
    WOS = [al(f"WOS{i}", [128, DT * 128], BF16, offset=C_OFF + 49152 + i * 4096) for i in range(2)]
    CWB = al("CWB", [128, TCH], F32, offset=C_OFF + 57344)
    SILU = al("SILU", [128, TCH], BF16, offset=C_OFF + 61440)
    FT2 = al("FT2", [128, TCH], BF16, offset=C_OFF + 63488)
    LNM2 = al("LNM2", [128, TCH], F32, offset=C_OFF + 61440)
    SQ2 = al("SQ2", [128, 2 * TCH], BF16, offset=C_OFF + 0)
    OUTS = [al(f"OUTS{i}", [128, TCH], F32, offset=C_OFF + 4096 + i * 4096) for i in range(2)]

    IDF = al("IDF", [128, 128], F32, offset=D_OFF + 0)
    ONESB = al("ONESB", [128, 128], BF16, offset=D_OFF + 512)
    ONESF = al("ONESF", [128, 128], F32, offset=D_OFF + 768)
    MASK = al("MASK", [128, 128], F32, offset=D_OFF + 1280)
    CST = al("CST", [128, 8], F32, offset=D_OFF + 1792)
    NLAM = al("NLAM", [128, 1], F32, offset=D_OFF + 1824)
    RK = al("RK", [128, 32], F32, offset=D_OFF + 1856)
    GKB = al("GKB", [128, DT * E], BF16, offset=D_OFF + 1984)
    GTMP = al("GTMP", [128, 16], F32, offset=D_OFF + 2240)
    GEQ = [al(f"GEQ{i}", [128, E], F32, offset=D_OFF + 2304 + i * 32) for i in range(3)]
    CWT = al("CWT", [8, T], F32, offset=D_OFF + 2400)
    ESEL = al("ESEL", [8, E * 128], F32, offset=D_OFF + 10592)

    PB = [nc.alloc_psum_tensor(f"PB{i}", [128, 512], F32) for i in range(4)]
    P2 = nc.alloc_psum_tensor("P2", [128, 2048], F32)
    BANKS = ["PB0", "PB1", "PB2", "PB3", "P2a", "P2b", "P2c", "P2d"]
    bank_dep = {b: None for b in BANKS}

    from contextlib import ExitStack
    ctx = ExitStack()
    mk = lambda name: Ctr(ctx.enter_context(nc.semaphore(name)), name)
    LDG = mk("LDG")
    LDS = mk("LDS")
    SWS = [mk("SWS0"), mk("SWS1")]
    SWI0 = [mk("SWI0a"), mk("SWI0b")]
    SWI1 = [mk("SWI1a"), mk("SWI1b")]
    SWO = [mk("SWOa"), mk("SWOb")]
    SOX = [mk("SOXa"), mk("SOXb")]
    SST = [mk("SSTa"), mk("SSTb")]
    DBG = mk("DBG")
    PT = mk("PT")
    PV = mk("PV")
    PS = mk("PS")

    blk = ctx.enter_context(nc.Block())

    known = {k: {} for k in "tvsgy"}

    def wt(eng, kind, dep):
        if dep is None:
            return
        sem, val = dep
        if val <= 0:
            return
        k = known[kind]
        if k.get(sem.name, -1) >= val:
            return
        eng.wait_ge(sem.h, val)
        k[sem.name] = val

    def wait_banks(t, names):
        for b in names:
            wt(t, "t", bank_dep[b])

    def rel_banks(names, dep):
        for b in names:
            bank_dep[b] = dep

    def TS(f):
        blk.tensor(f)

    def VS(f):
        blk.vector(f)

    def SS(f):
        blk.scalar(f)

    def GS(f):
        blk.gpsimd(f)

    def YS(f):
        blk.sync(f)

    def _g(g):
        g.dma_start(out=ONESB[:], in_=onesb_in).then_inc(LDG.h, 16); LDG.n += 16
        g.dma_start(out=GKB[:].rearrange("p (n m) -> p n m", n=DT),
                    in_=gk.rearrange("(n p) m -> p n m", p=128)).then_inc(LDG.h, 16); LDG.n += 16
        for dt in range(DT):
            g.dma_start(out=XT[:, dt * T:(dt + 1) * T],
                        in_=xt[dt * 128:(dt + 1) * 128, :]).then_inc(LDG.h, 16); LDG.n += 16
    GS(_g)
    ld_xt = LDG.n
    ld_gkb = 32

    def _y(y):
        for dst, src in ((IDF, idf), (ONESF, onesf_in), (MASK, maskf), (CST, cst_in),
                         (LAMV, lam), (ESEL, esel_in)):
            y.dma_start(out=dst[:], in_=src).then_inc(LDS.h, 16); LDS.n += 16
    YS(_y)
    ld_consts = LDS.n

    lam_col = None

    def _v(v):
        wt(v, "v", (LDS, ld_consts))
        v.tensor_mul(LNT[0:1, 0:256], LAMV[0:1, 0:HD], LAMV[0:1, HD:2 * HD])
        i = PV.inc(v.tensor_mul(LNT[0:1, 256:512], LAMV[0:1, 2 * HD:3 * HD], LAMV[0:1, 3 * HD:4 * HD]))
        wt(v, "v", (PV, i))
        v.reduce_sum(GTMP[0:1, 0:1], LNT[0:1, 0:256], axis=AX.X)
        _v.out = PV.inc(v.reduce_sum(GTMP[0:1, 1:2], LNT[0:1, 256:512], axis=AX.X))
    VS(_v); v_sums = _v.out

    def _s(s):
        wt(s, "s", (PV, v_sums))
        s.activation(GTMP[0:1, 2:3], GTMP[0:1, 0:1], AF.Exp)
        _s.out = PS.inc(s.activation(GTMP[0:1, 3:4], GTMP[0:1, 1:2], AF.Exp))
    SS(_s); s_exps = _s.out

    def _v(v):
        wt(v, "v", (PS, s_exps))
        i = PV.inc(v.tensor_sub(GTMP[0:1, 4:5], GTMP[0:1, 3:4], GTMP[0:1, 2:3]))
        wt(v, "v", (PV, i))
        _v.out = PV.inc(v.tensor_scalar(GTMP[0:1, 5:6], GTMP[0:1, 4:5], 1.0, -LAMBDA_INIT, ALU.mult, ALU.add))
    VS(_v); v_neg = _v.out

    def _t(t):
        wt(t, "t", (PV, v_neg))
        wt(t, "t", (LDS, ld_consts))
        wait_banks(t, ["PB0"])
        _t.out = PT.inc(t.matmul(PB[0][:, 0:1], ONESF[0:1, :], GTMP[0:1, 5:6], start=True, stop=True))
    TS(_t); t_bc = _t.out

    def _s(s):
        wt(s, "s", (PT, t_bc))
        _s.out = PS.inc(s.copy(NLAM[:], PB[0][:, 0:1]))
    SS(_s); lam_col = (PS, _s.out)
    bank_dep["PB0"] = lam_col

    strip_cons = [None, None]
    att_cons = [None] * 16
    p2_dep_all = ["P2a", "P2b", "P2c", "P2d"]
    y_done = None
    evac_flip = 0
    x1_done = None
    sq_cons = [0]
    qkv_war = []
    acc_state_jt = [None] * TT

    if emit_attn:
        strip_seq = [(m, h) for h in range(n_heads) for m in (wqT, wkT, wvT)]
        strip_val = {}
        next_strip = [0]

        def load_strip():
            i = next_strip[0]
            if i >= len(strip_seq):
                return
            mat, h = strip_seq[i]
            slot = i % 2

            def _g(g):
                wt(g, "g", strip_cons[slot])
                g.dma_start(out=WS[slot][:].rearrange("p (n m) -> p n m", n=DT),
                            in_=mat[:, h * HD:(h + 1) * HD].rearrange("(n p) m -> p n m", p=128)
                            ).then_inc(SWS[slot].h, 16)
                SWS[slot].n += 16
            GS(_g)
            strip_val[i] = SWS[slot].n
            next_strip[0] += 1

        load_strip()
        load_strip()

        for h in range(n_heads):
            for pj in range(3):
                si = h * 3 + pj
                slot = si % 2
                sval = strip_val[si]
                if pj < 2:
                    dstbuf = QT if pj == 0 else KT
                    for grp in range(8):
                        jt, tc = grp // 4, grp % 4
                        bn = grp % 2

                        def _t(t, jt=jt, tc=tc, bn=bn, slot=slot, sval=sval):
                            wt(t, "t", (LDG, ld_xt))
                            wt(t, "t", (SWS[slot], sval))
                            wait_banks(t, [BANKS[bn]])
                            for dt in range(DT):
                                mm = t.matmul(PB[bn][:],
                                              WS[slot][:, dt * HD + jt * 128: dt * HD + (jt + 1) * 128],
                                              XT[:, dt * T + tc * 512: dt * T + (tc + 1) * 512],
                                              start=(dt == 0), stop=(dt == DT - 1))
                            _t.out = PT.inc(mm)
                        TS(_t)
                        mmv = _t.out
                        dst = dstbuf[:, jt * T + tc * 512: jt * T + (tc + 1) * 512]
                        if evac_flip % 2 == 0:
                            def _s(s, dst=dst, bn=bn, mmv=mmv):
                                wt(s, "s", (PT, mmv))
                                for dep in qkv_war:
                                    wt(s, "s", dep)
                                _s.out = PS.inc(s.copy(dst, PB[bn][:]))
                            SS(_s)
                            rel_banks([BANKS[bn]], (PS, _s.out))
                        else:
                            def _v(v, dst=dst, bn=bn, mmv=mmv):
                                wt(v, "v", (PT, mmv))
                                for dep in qkv_war:
                                    wt(v, "v", dep)
                                _v.out = PV.inc(v.tensor_copy(dst, PB[bn][:]))
                            VS(_v)
                            rel_banks([BANKS[bn]], (PV, _v.out))
                        evac_flip += 1
                else:
                    for tt in range(TT):
                        bn = 2 + tt % 2

                        def _t(t, tt=tt, bn=bn, slot=slot, sval=sval):
                            wt(t, "t", (LDG, ld_xt))
                            wt(t, "t", (SWS[slot], sval))
                            wait_banks(t, [BANKS[bn]])
                            for dt in range(DT):
                                mm = t.matmul(PB[bn][:, 0:HD],
                                              XT[:, dt * T + tt * 128: dt * T + (tt + 1) * 128],
                                              WS[slot][:, dt * HD:(dt + 1) * HD],
                                              start=(dt == 0), stop=(dt == DT - 1))
                            _t.out = PT.inc(mm)
                        TS(_t)
                        mmv = _t.out
                        dst = VTOK[:, tt * HD:(tt + 1) * HD]
                        if evac_flip % 2 == 0:
                            def _s(s, dst=dst, bn=bn, mmv=mmv):
                                wt(s, "s", (PT, mmv))
                                for dep in qkv_war:
                                    wt(s, "s", dep)
                                _s.out = PS.inc(s.copy(dst, PB[bn][:, 0:HD]))
                            SS(_s)
                            rel_banks([BANKS[bn]], (PS, _s.out))
                        else:
                            def _v(v, dst=dst, bn=bn, mmv=mmv):
                                wt(v, "v", (PT, mmv))
                                for dep in qkv_war:
                                    wt(v, "v", dep)
                                _v.out = PV.inc(v.tensor_copy(dst, PB[bn][:, 0:HD]))
                            VS(_v)
                            rel_banks([BANKS[bn]], (PV, _v.out))
                        evac_flip += 1
                strip_cons[slot] = (PT, PT.n)
                load_strip()

            proj_done = [bank_dep[b] for b in BANKS[:4]]

            for m in range(2):
                for side in range(2):
                    buf = QT if side == 0 else KT
                    def _vsq(v, m=m, side=side, buf=buf):
                        if side == 0:
                            for dep in proj_done:
                                wt(v, "v", dep)
                        wt(v, "v", (PT, sq_cons[0]))
                        _vsq.out = PV.inc(v.tensor_mul(SQ[:], buf[:, m * T:(m + 1) * T],
                                                       buf[:, m * T:(m + 1) * T]))
                    VS(_vsq)
                    sqv = _vsq.out
                    for c in range(4):
                        pb_name = "P2b" if c % 2 == 0 else "P2c"
                        psl = P2[:, 512:1024] if c % 2 == 0 else P2[:, 1024:1536]
                        def _t(t, c=c, sqv=sqv, pb_name=pb_name, psl=psl):
                            wt(t, "t", (PV, sqv))
                            wait_banks(t, [pb_name])
                            _t.out = PT.inc(t.matmul(psl, ONESB[:],
                                                     SQ[:, c * 512:(c + 1) * 512],
                                                     start=True, stop=True))
                        TS(_t)
                        mmv = _t.out
                        def _s(s, side=side, mmv=mmv, psl=psl):
                            wt(s, "s", (PT, mmv))
                            i = PS.inc(s.activation(psl, psl, AF.Ln, scale=1.0 / HALF,
                                                    bias=CST[:, 0:1]))
                            wt(s, "s", (PS, i))
                            b2 = CST[:, 2:3] if side == 0 else CST[:, 3:4]
                            _s.out = PS.inc(s.activation(psl, psl, AF.Exp, scale=-0.5, bias=b2))
                        SS(_s)
                        rv = _s.out
                        def _v(v, m=m, c=c, buf=buf, rv=rv, psl=psl):
                            wt(v, "v", (PS, rv))
                            _v.out = PV.inc(v.tensor_tensor(
                                buf[:, m * T + c * 512: m * T + (c + 1) * 512],
                                buf[:, m * T + c * 512: m * T + (c + 1) * 512], psl, ALU.mult))
                        VS(_v)
                        rel_banks([pb_name], (PV, _v.out))
                    if side == 0:
                        q1n_val = _v.out
                    else:
                        k1n_val = _v.out
                    sq_cons[0] = mmv

                if h == 0 and m == 1 and ("d_qt" in dbg or "d_kt" in dbg or "d_vt" in dbg):
                    def _g(g, k1n_val=k1n_val):
                        wt(g, "g", (PV, k1n_val))
                        if "d_qt" in dbg:
                            g.dma_start(out=dbg["d_qt"], in_=QT[:]).then_inc(DBG.h, 16); DBG.n += 16
                        if "d_kt" in dbg:
                            g.dma_start(out=dbg["d_kt"], in_=KT[:]).then_inc(DBG.h, 16); DBG.n += 16
                        if "d_vt" in dbg:
                            g.dma_start(out=dbg["d_vt"], in_=VTOK[:]).then_inc(DBG.h, 16); DBG.n += 16
                    GS(_g)

                for qb in range(NQB):
                    nkt = 4 * qb + 4
                    exp_done = [0] * nkt
                    for kt in range(nkt):
                        qs = max(0, kt * 128 - qb * 512)
                        bn = kt % 2

                        if "d_sc0" in dbg and h == 0 and m == 0 and qb == 3 and kt == 0:
                            def _sd0(s):
                                wt(s, "s", bank_dep["PB0"])
                                _sd0.out = PS.inc(s.copy(RB[:], PB[0][:]))
                            SS(_sd0)
                            def _gd0(g):
                                wt(g, "g", (PS, _sd0.out))
                                g.dma_start(out=dbg["d_sc0"], in_=RB[:]).then_inc(DBG.h, 16)
                                DBG.n += 16
                            GS(_gd0)
                            pre_dump_g = (DBG, DBG.n)
                            def _vw0(v, d=pre_dump_g):
                                wt(v, "v", d)
                            VS(_vw0)
                            bank_dep["PB0"] = (PS, _sd0.out)

                        def _t(t, kt=kt, qb=qb, qs=qs, bn=bn, m=m, k1n_val=k1n_val):
                            wt(t, "t", (PV, k1n_val))
                            wait_banks(t, [BANKS[bn]])
                            if "d_sc0" in dbg and h == 0 and m == 0 and qb == 3 and kt == 0:
                                wt(t, "t", (DBG, DBG.n))
                            _t.out = PT.inc(t.matmul(
                                PB[bn][:, qs:512],
                                KT[:, m * T + kt * 128: m * T + (kt + 1) * 128],
                                QT[:, m * T + qb * 512 + qs: m * T + (qb + 1) * 512],
                                start=True, stop=True))
                        TS(_t)
                        sc_val = _t.out

                        if "d_sc0" in dbg and h == 0 and m == 0 and qb == 3 and kt == 0:
                            pass
                        if "d_sc" in dbg and h == 0 and m == 0 and qb == 3 and kt == 0:
                            DSTG = al("DSTG", [128, 512], F32, offset=C_OFF + 36864)
                            def _vset(v):
                                _vset.out = PV.inc(v.memset(DSTG[:], -7.0))
                            VS(_vset)
                            def _sdump(s, bn=bn, sc_val=sc_val):
                                wt(s, "s", (PT, sc_val))
                                wt(s, "s", (PV, _vset.out))
                                _sdump.out = PS.inc(s.copy(DSTG[:], PB[bn][:]))
                            SS(_sdump)
                            def _gdump(g):
                                wt(g, "g", (PS, _sdump.out))
                                g.dma_start(out=dbg["d_sc"], in_=DSTG[:]).then_inc(DBG.h, 16)
                                DBG.n += 16
                            GS(_gdump)

                        mask_val = 0
                        if kt >= 4 * qb:
                            dio = kt * 128 - qb * 512

                            def _v(v, dio=dio, bn=bn, sc_val=sc_val):
                                wt(v, "v", (PT, sc_val))
                                _v.out = PV.inc(v.tensor_tensor(PB[bn][:, dio:dio + 128],
                                                                PB[bn][:, dio:dio + 128],
                                                                MASK[:], ALU.add))
                            VS(_v)
                            mask_val = _v.out

                        def _s(s, kt=kt, qs=qs, bn=bn, m=m, sc_val=sc_val, mask_val=mask_val):
                            wt(s, "s", (PT, sc_val))
                            if mask_val:
                                wt(s, "s", (PV, mask_val))
                            wt(s, "s", att_cons[kt])
                            if 8 <= kt < 12:
                                wt(s, "s", (PV, v_sums))
                            _s.out = PS.inc(s.activation(
                                ATT[:, kt * 512 + qs: (kt + 1) * 512],
                                PB[bn][:, qs:512], AF.Exp))
                        SS(_s)
                        exp_done[kt] = _s.out
                        rel_banks([BANKS[bn]], (PS, _s.out))

                    def _t(t, qb=qb, nkt=nkt, exp_done=exp_done):
                        wait_banks(t, ["P2a", "PB2", "PB3"])
                        for kt in range(nkt):
                            qs = max(0, kt * 128 - qb * 512)
                            wt(t, "t", (PS, exp_done[kt]))
                            t.matmul(P2[:, qs:512], ONESB[:],
                                     ATT[:, kt * 512 + qs: (kt + 1) * 512],
                                     start=(kt == 0), stop=(kt == nkt - 1))
                        for ct in range(2):
                            for kt in range(nkt):
                                qs = max(0, kt * 128 - qb * 512)
                                mm = t.matmul(PB[2 + ct][:, qs:512],
                                              VTOK[:, kt * HD + ct * 128: kt * HD + (ct + 1) * 128],
                                              ATT[:, kt * 512 + qs: (kt + 1) * 512],
                                              start=(kt == 0), stop=(kt == nkt - 1))
                        _t.out = PT.inc(mm)
                    TS(_t)
                    av_val = _t.out
                    for kt in range(nkt):
                        att_cons[kt] = (PT, av_val)

                    def _v(v, qb=qb, m=m, h=h, av_val=av_val):
                        wt(v, "v", (PT, av_val))
                        i = PV.inc(v.reciprocal(RB[:], P2[:, 0:512]))
                        wt(v, "v", (PV, i))
                        if m == 0:
                            for ct in range(2):
                                i = PV.inc(v.tensor_tensor(
                                    Y[:, (2 * h + ct) * T + qb * 512: (2 * h + ct) * T + (qb + 1) * 512],
                                    PB[2 + ct][:], RB[:], ALU.mult))
                        else:
                            wt(v, "v", lam_col)
                            for ct in range(2):
                                i0 = PV.inc(v.tensor_tensor(TMP[:], PB[2 + ct][:], RB[:], ALU.mult))
                                wt(v, "v", (PV, i0))
                                ysl = Y[:, (2 * h + ct) * T + qb * 512: (2 * h + ct) * T + (qb + 1) * 512]
                                i = PV.inc(v.scalar_tensor_tensor(ysl, TMP[:], NLAM[:, 0:1], ysl,
                                                                  ALU.mult, ALU.add))
                                wt(v, "v", (PV, i))
                        _v.out = i
                    VS(_v)
                    yv = _v.out
                    rel_banks(["PB2", "PB3", "P2a"], (PV, yv))

                    if h == 0 and m == 0 and qb == 3 and ("d_att" in dbg or "d_rb" in dbg):
                        def _g(g, yv=yv):
                            wt(g, "g", (PV, yv))
                            if "d_att" in dbg:
                                g.dma_start(out=dbg["d_att"], in_=ATT[:]).then_inc(DBG.h, 16)
                                DBG.n += 16
                            if "d_rb" in dbg:
                                g.dma_start(out=dbg["d_rb"], in_=RB[:]).then_inc(DBG.h, 16)
                                DBG.n += 16
                        GS(_g)
                        for _kt in range(16):
                            att_cons[_kt] = (DBG, DBG.n)

            qkv_war.clear()
            qkv_war.extend([(PT, PT.n), (PS, PS.n), (PV, PV.n)])

        y_done = (PV, PV.n)
        dump_guard = [None]
        if "d_y" in dbg:
            def _g(g):
                wt(g, "g", y_done)
                for ct in range(DT):
                    g.dma_start(out=dbg["d_y"][ct * 128:(ct + 1) * 128, :],
                                in_=Y[:, ct * T:(ct + 1) * T]).then_inc(DBG.h, 16)
                    DBG.n += 16
            GS(_g)
            dump_guard[0] = (DBG, DBG.n)

        res1_done = [None] * TT
        oxf_cons = [None, None]
        for jb in range(8):
            slot = jb % 2

            def _g(g, jb=jb, slot=slot):
                wt(g, "g", strip_cons[slot])
                g.dma_start(out=WS[slot][:].rearrange("p (n m) -> p n m", n=DT),
                            in_=woT[:, jb * HD:(jb + 1) * HD].rearrange("(n p) m -> p n m", p=128)
                            ).then_inc(SWS[slot].h, 16)
                SWS[slot].n += 16
            GS(_g)
            wval = SWS[slot].n
            for ji in range(2):
                jt = jb * 2 + ji
                xslot = jt % 2

                def _y(y, jt=jt, xslot=xslot):
                    wt(y, "y", y_done)
                    wt(y, "y", oxf_cons[xslot])
                    y.dma_start(out=OXF[xslot][:], in_=xt[jt * 128:(jt + 1) * 128, :]
                                ).then_inc(SOX[xslot].h, 16)
                    SOX[xslot].n += 16
                YS(_y)
                xval = SOX[xslot].n
                last = None
                for tc in range(4):
                    bn = tc % 2

                    def _t(t, ji=ji, tc=tc, bn=bn, slot=slot, wval=wval):
                        wt(t, "t", (SWS[slot], wval))
                        wt(t, "t", y_done)
                        wait_banks(t, [BANKS[bn]])
                        for ct in range(DT):
                            mm = t.matmul(PB[bn][:],
                                          WS[slot][:, ct * HD + ji * 128: ct * HD + (ji + 1) * 128],
                                          Y[:, ct * T + tc * 512: ct * T + (tc + 1) * 512],
                                          start=(ct == 0), stop=(ct == DT - 1))
                        _t.out = PT.inc(mm)
                    TS(_t)
                    mmv = _t.out

                    def _v(v, jt=jt, tc=tc, bn=bn, xslot=xslot, xval=xval, mmv=mmv):
                        wt(v, "v", (PT, mmv))
                        wt(v, "v", (SOX[xslot], xval))
                        _v.out = PV.inc(v.scalar_tensor_tensor(
                            RES1[:, jt * T + tc * 512: jt * T + (tc + 1) * 512],
                            PB[bn][:], 1.0 - LAMBDA_INIT,
                            OXF[xslot][:, tc * 512:(tc + 1) * 512],
                            ALU.mult, ALU.add))
                    VS(_v)
                    last = (PV, _v.out)
                    rel_banks([BANKS[bn]], last)
                oxf_cons[xslot] = last
                res1_done[jt] = last
            strip_cons[slot] = (PT, PT.n)

        if "d_res1" in dbg:
            def _g(g):
                for jt in range(TT):
                    wt(g, "g", res1_done[jt])
                    g.dma_start(out=dbg["d_res1"][jt * 128:(jt + 1) * 128, :],
                                in_=RES1[:, jt * T:(jt + 1) * T]).then_inc(DBG.h, 16)
                    DBG.n += 16
            GS(_g)

        attn_guard = (PT, PT.n)
        sqd = [None, None]
        stats_val = None
        for jt in range(TT):
            slot = jt % 2

            def _vv(v, jt=jt, slot=slot):
                wt(v, "v", res1_done[jt])
                wt(v, "v", sqd[slot])
                _vv.out = PV.inc(v.tensor_mul(LNSQ[slot][:], RES1[:, jt * T:(jt + 1) * T],
                                              RES1[:, jt * T:(jt + 1) * T]))
            VS(_vv)
            sqv = _vv.out

            def _t(t, jt=jt, slot=slot, sqv=sqv):
                if jt == 0:
                    wait_banks(t, BANKS[:4] + p2_dep_all)
                wt(t, "t", res1_done[jt])
                wt(t, "t", (PV, sqv))
                for c in range(4):
                    t.matmul(PB[c][:], ONESB[:],
                             RES1[:, jt * T + c * 512: jt * T + (c + 1) * 512],
                             start=(jt == 0), stop=(jt == TT - 1))
                for c in range(4):
                    mm = t.matmul(P2[:, c * 512:(c + 1) * 512], ONESB[:],
                                  LNSQ[slot][:, c * 512:(c + 1) * 512],
                                  start=(jt == 0), stop=(jt == TT - 1))
                _t.out = PT.inc(mm)
            TS(_t)
            sqd[slot] = (PT, _t.out)
            stats_val = _t.out

        mr_val = None
        for c in range(4):
            def _v(v, c=c):
                wt(v, "v", (PT, stats_val))
                i = PV.inc(v.tensor_scalar(LNM[:, c * 512:(c + 1) * 512], PB[c][:],
                                           1.0 / T, None, ALU.mult))
                wt(v, "v", (PV, i))
                v.tensor_scalar(P2[:, c * 512:(c + 1) * 512], P2[:, c * 512:(c + 1) * 512],
                                1.0 / T, None, ALU.mult)
                i = PV.inc(v.tensor_tensor(LNT[:], LNM[:, c * 512:(c + 1) * 512],
                                           LNM[:, c * 512:(c + 1) * 512], ALU.mult))
                wt(v, "v", (PV, i))
                _v.out = PV.inc(v.tensor_tensor(P2[:, c * 512:(c + 1) * 512],
                                                P2[:, c * 512:(c + 1) * 512], LNT[:], ALU.subtract))
            VS(_v)
            var_val = _v.out

            def _s(s, c=c, var_val=var_val):
                wt(s, "s", (PV, var_val))
                i = PS.inc(s.activation(P2[:, c * 512:(c + 1) * 512], P2[:, c * 512:(c + 1) * 512],
                                        AF.Ln, bias=CST[:, 1:2]))
                wt(s, "s", (PS, i))
                _s.out = PS.inc(s.activation(P2[:, c * 512:(c + 1) * 512],
                                             P2[:, c * 512:(c + 1) * 512], AF.Exp, scale=-0.5, bias=CST[:, 3:4]))
            SS(_s)
            rstd_val = _s.out

            def _v(v, c=c, rstd_val=rstd_val):
                wt(v, "v", (PS, rstd_val))
                _v.out = PV.inc(v.tensor_tensor(LNM[:, c * 512:(c + 1) * 512],
                                                LNM[:, c * 512:(c + 1) * 512],
                                                P2[:, c * 512:(c + 1) * 512], ALU.mult))
            VS(_v)
            mr_val = _v.out

        for jt in range(TT):
            def _v(v, jt=jt, mr_val=mr_val):
                wt(v, "v", (PV, mr_val))
                if dump_guard[0] is not None:
                    wt(v, "v", dump_guard[0])
                for c in range(4):
                    i0 = PV.inc(v.tensor_tensor(TMP[:],
                                                RES1[:, jt * T + c * 512: jt * T + (c + 1) * 512],
                                                P2[:, c * 512:(c + 1) * 512], ALU.mult))
                    wt(v, "v", (PV, i0))
                    i = PV.inc(v.tensor_tensor(X1T[:, jt * T + c * 512: jt * T + (c + 1) * 512],
                                               TMP[:], LNM[:, c * 512:(c + 1) * 512], ALU.subtract))
                    wt(v, "v", (PV, i))
                _v.out = i
            VS(_v)
            x1_done = (PV, _v.out)
        rel_banks(BANKS[:4] + p2_dep_all, x1_done)
    else:
        attn_guard = (PT, PT.n)
        for jt in range(TT):
            def _v(v, jt=jt):
                wt(v, "v", (LDG, ld_xt))
                _v.out = PV.inc(v.tensor_copy(X1T[:, jt * T:(jt + 1) * T], XT[:, jt * T:(jt + 1) * T]))
            VS(_v)
            x1_done = (PV, _v.out)

    if "d_x1" in dbg:
        def _g(g):
            wt(g, "g", x1_done)
            for jt in range(TT):
                g.dma_start(out=dbg["d_x1"][jt * 128:(jt + 1) * 128, :],
                            in_=X1T[:, jt * T:(jt + 1) * T]).then_inc(DBG.h, 16)
                DBG.n += 16
        GS(_g)

    if not emit_ffn:
        def _g(g):
            wt(g, "g", x1_done)
            for jt in range(TT):
                g.dma_start(out=outT[jt * 128:(jt + 1) * 128, :],
                            in_=X1T[:, jt * T:(jt + 1) * T]).then_inc(DBG.h, 16)
                DBG.n += 16
            g.wait_ge(DBG.h, DBG.n)
        GS(_g)
        ctx.close()
        return nc, dbg

    def _t(t):
        wt(t, "t", x1_done)
        wt(t, "t", (LDG, ld_xt))
        wait_banks(t, p2_dep_all)
        for dt in range(DT):
            for c in range(4):
                mm = t.matmul(P2[0:8, c * 512:(c + 1) * 512],
                              GKB[:, dt * E:(dt + 1) * E],
                              X1T[:, dt * T + c * 512: dt * T + (c + 1) * 512],
                              start=(dt == 0), stop=(dt == DT - 1))
        _t.out = PT.inc(mm)
    TS(_t)
    glog_val = _t.out

    def _s(s):
        wt(s, "s", (PT, glog_val))
        _s.out = PS.inc(s.copy(LOGIT[:], P2[0:8, :]))
    SS(_s)
    logit_val = _s.out
    rel_banks(p2_dep_all, (PS, logit_val))

    cw_val = None
    gtok_val = 0
    cwev_val = 0
    for tt in range(TT):
        def _t(t, tt=tt):
            wt(t, "t", (PS, logit_val))
            if tt == 0:
                wait_banks(t, ["PB0"])
            else:
                wt(t, "t", (PV, gtok_val))
            _t.out = PT.inc(t.transpose(PB[0][:, 0:8], LOGIT[0:8, tt * 128:(tt + 1) * 128],
                                        IDF[0:8, 0:8]))
        TS(_t)
        gtr_val = _t.out

        def _v(v, tt=tt, gtr_val=gtr_val):
            wt(v, "v", (PT, gtr_val))
            _v.out = PV.inc(v.tensor_copy(CWTOK[:, tt * E:(tt + 1) * E], PB[0][:, 0:8]))
        VS(_v)
        gtok_val = _v.out

    for tt in range(TT):
        lg = CWTOK[:, tt * E:(tt + 1) * E]

        def _v(v, lg=lg):
            wt(v, "v", (PV, gtok_val))
            i = PV.inc(v.reduce_max(GTMP[:, 0:1], lg, axis=AX.X))
            wt(v, "v", (PV, i))
            i = PV.inc(v.tensor_scalar(GEQ[0][:], lg, GTMP[:, 0:1], None, ALU.is_equal))
            wt(v, "v", (PV, i))
            i = PV.inc(v.scalar_tensor_tensor(GEQ[1][:], GEQ[0][:], -1e30, lg, ALU.mult, ALU.add))
            wt(v, "v", (PV, i))
            i = PV.inc(v.reduce_max(GTMP[:, 1:2], GEQ[1][:], axis=AX.X))
            wt(v, "v", (PV, i))
            _v.out = PV.inc(v.tensor_sub(GTMP[:, 2:3], GTMP[:, 1:2], GTMP[:, 0:1]))
        VS(_v)
        g1_val = _v.out

        def _s(s, g1_val=g1_val):
            wt(s, "s", (PV, g1_val))
            _s.out = PS.inc(s.activation(GTMP[:, 3:4], GTMP[:, 2:3], AF.Exp))
        SS(_s)
        g2_val = _s.out

        def _v(v, lg=lg, g2_val=g2_val):
            wt(v, "v", (PS, g2_val))
            i = PV.inc(v.tensor_scalar(GTMP[:, 4:5], GTMP[:, 3:4], 1.0, 1.0, ALU.mult, ALU.add))
            wt(v, "v", (PV, i))
            i = PV.inc(v.reciprocal(GTMP[:, 5:6], GTMP[:, 4:5]))
            wt(v, "v", (PV, i))
            i = PV.inc(v.tensor_tensor(GTMP[:, 6:7], GTMP[:, 3:4], GTMP[:, 5:6], ALU.mult))
            wt(v, "v", (PV, i))
            i = PV.inc(v.tensor_scalar(GTMP[:, 7:8], GTMP[:, 6:7], -1.0, 1.0, ALU.mult, ALU.add))
            wt(v, "v", (PV, i))
            i = PV.inc(v.tensor_scalar(GEQ[2][:], GEQ[1][:], GTMP[:, 1:2], None, ALU.is_equal))
            wt(v, "v", (PV, i))
            i = PV.inc(v.tensor_scalar(GEQ[0][:], GEQ[0][:], GTMP[:, 7:8], None, ALU.mult))
            wt(v, "v", (PV, i))
            _v.out = PV.inc(v.scalar_tensor_tensor(lg, GEQ[2][:], GTMP[:, 6:7], GEQ[0][:],
                                                   ALU.mult, ALU.add))
        VS(_v)
        cw_val = _v.out

        def _t(t, tt=tt, lg=lg, cw_val=cw_val):
            wt(t, "t", (PV, cw_val))
            if tt == 0:
                wait_banks(t, ["PB1"])
            else:
                wt(t, "t", (PS, cwev_val))
            _t.out = PT.inc(t.transpose(PB[1][0:8, 0:128], lg, IDF[:]))
        TS(_t)
        cwtr_val = _t.out

        def _s(s, tt=tt, cwtr_val=cwtr_val):
            wt(s, "s", (PT, cwtr_val))
            _s.out = PS.inc(s.copy(CWT[:, tt * 128:(tt + 1) * 128], PB[1][0:8, 0:128]))
        SS(_s)
        cwev_val = _s.out
    cw_done = (PS, cwev_val)
    rel_banks(["PB0"], (PV, gtok_val))
    rel_banks(["PB1"], cw_done)

    if "d_cw" in dbg:
        def _y(y):
            wt(y, "y", cw_done)
            y.dma_start(out=dbg["d_cw"], in_=CWT[:]).then_inc(DBG.h, 16)
            DBG.n += 16
        YS(_y)

    gate_guard = attn_guard
    h0p = [[("PB0", PB[0][:]), ("PB1", PB[1][:])],
           [("P2a", P2[:, 0:512]), ("P2b", P2[:, 512:1024])]]
    h1p = [[("PB2", PB[2][:]), ("PB3", PB[3][:])],
           [("P2c", P2[:, 1024:1536]), ("P2d", P2[:, 1536:2048])]]
    wi_cons = [[gate_guard, gate_guard], [gate_guard, gate_guard]]
    wo_cons = [gate_guard, gate_guard]
    h_cons = None
    acc_read = [None] * TT
    outs_cons = [None, None]
    silu_cons = None
    ft2_cons = None
    cwb_cons = None

    half_guard = []
    for hf in range(2):
        toff = hf * TCH
        for e in range(E):
            def _t(t, e=e, hf=hf):
                wt(t, "t", cw_done)
                for dep in half_guard:
                    wt(t, "t", dep)
                wait_banks(t, ["P2a", "P2b"])
                for c in range(2):
                    mm = t.matmul(P2[:, c * 512:(c + 1) * 512], ESEL[:, e * 128:(e + 1) * 128],
                                  CWT[:, hf * TCH + c * 512: hf * TCH + (c + 1) * 512],
                                  start=True, stop=True)
                _t.out = PT.inc(mm)
            TS(_t)
            cwbmm_val = _t.out

            def _v(v, cwbmm_val=cwbmm_val):
                wt(v, "v", (PT, cwbmm_val))
                wt(v, "v", cwb_cons)
                for dep in half_guard:
                    wt(v, "v", dep)
                _v.out = PV.inc(v.tensor_copy(CWB[:], P2[:, 0:1024]))
            VS(_v)
            cwb_val = _v.out
            rel_banks(["P2a", "P2b"], (PV, cwb_val))

            hev_val = None
            for mt in range(TT):
                par = mt % 2

                def _g(g, e=e, mt=mt, par=par):
                    for wsel, bufs, mat, sem in ((0, WI0S, wi0, SWI0[par]), (1, WI1S, wi1, SWI1[par])):
                        wt(g, "g", wi_cons[wsel][par])
                        g.dma_start(out=bufs[par][:].rearrange("p (n m) -> p n m", n=DT),
                                    in_=mat[e, :, mt * 128:(mt + 1) * 128]
                                    .rearrange("(n p) m -> p n m", p=128)).then_inc(sem.h, 16)
                        sem.n += 16
                GS(_g)
                wival = (SWI0[par].n, SWI1[par].n)

                def _t(t, mt=mt, par=par, wival=wival, toff=toff):
                    wt(t, "t", (SWI0[par], wival[0]))
                    wt(t, "t", (SWI1[par], wival[1]))
                    wt(t, "t", x1_done)
                    wait_banks(t, [h0p[par][0][0], h0p[par][1][0], h1p[par][0][0], h1p[par][1][0]])
                    for dt in range(DT):
                        for c in range(2):
                            t.matmul(h0p[par][c][1],
                                     WI0S[par][:, dt * 128:(dt + 1) * 128],
                                     X1T[:, dt * T + toff + c * 512: dt * T + toff + (c + 1) * 512],
                                     start=(dt == 0), stop=(dt == DT - 1))
                    for dt in range(DT):
                        for c in range(2):
                            mm = t.matmul(h1p[par][c][1],
                                          WI1S[par][:, dt * 128:(dt + 1) * 128],
                                          X1T[:, dt * T + toff + c * 512: dt * T + toff + (c + 1) * 512],
                                          start=(dt == 0), stop=(dt == DT - 1))
                    _t.out = PT.inc(mm)
                TS(_t)
                hmm_val = _t.out
                wi_cons[0][par] = (PT, hmm_val)
                wi_cons[1][par] = (PT, hmm_val)

                def _s(s, par=par, hmm_val=hmm_val):
                    wt(s, "s", (PT, hmm_val))
                    wt(s, "s", silu_cons)
                    for dep in half_guard:
                        wt(s, "s", dep)
                    for c in range(2):
                        i = PS.inc(s.activation(SILU[:, c * 512:(c + 1) * 512],
                                                h0p[par][c][1], AF.Silu))
                    _s.out = i
                SS(_s)
                silu_val = _s.out

                def _v(v, mt=mt, par=par, hmm_val=hmm_val, silu_val=silu_val, cwb_val=cwb_val):
                    wt(v, "v", (PS, silu_val))
                    wt(v, "v", (PT, hmm_val))
                    wt(v, "v", (PV, cwb_val))
                    wt(v, "v", h_cons)
                    wt(v, "v", ft2_cons)
                    for c in range(2):
                        i0 = PV.inc(v.tensor_tensor(FT2[:, c * 512:(c + 1) * 512],
                                                    SILU[:, c * 512:(c + 1) * 512],
                                                    h1p[par][c][1], ALU.mult))
                    wt(v, "v", (PV, i0))
                    _v.out = PV.inc(v.tensor_tensor(H[:, mt * TCH:(mt + 1) * TCH],
                                                    FT2[:], CWB[:], ALU.mult))
                VS(_v)
                hev_val = _v.out
                silu_cons = (PV, hev_val)
                ft2_cons = (PV, hev_val)
                dep = (PV, hev_val)
                rel_banks([h0p[par][0][0], h0p[par][1][0], h1p[par][0][0], h1p[par][1][0]], dep)
            cwb_cons = (PV, hev_val)
            all_h = hev_val

            for jt in range(TT):
                slot = jt % 2

                def _g(g, e=e, jt=jt, slot=slot):
                    wt(g, "g", wo_cons[slot])
                    g.dma_start(out=WOS[slot][:].rearrange("p (n m) -> p n m", n=DT),
                                in_=wom[e, :, jt * 128:(jt + 1) * 128]
                                .rearrange("(n p) m -> p n m", p=128)).then_inc(SWO[slot].h, 16)
                    SWO[slot].n += 16
                GS(_g)
                woval = SWO[slot].n
                for c in range(2):
                    bn = c

                    def _t(t, c=c, bn=bn, slot=slot, woval=woval, all_h=all_h):
                        wt(t, "t", (SWO[slot], woval))
                        wt(t, "t", (PV, all_h))
                        wait_banks(t, [BANKS[bn]])
                        for mt in range(TT):
                            mm = t.matmul(PB[bn][:],
                                          WOS[slot][:, mt * 128:(mt + 1) * 128],
                                          H[:, mt * TCH + c * 512: mt * TCH + (c + 1) * 512],
                                          start=(mt == 0), stop=(mt == TT - 1))
                        _t.out = PT.inc(mm)
                    TS(_t)
                    omm_val = _t.out

                    def _v(v, e=e, jt=jt, c=c, bn=bn, omm_val=omm_val):
                        wt(v, "v", (PT, omm_val))
                        dst = ACC[:, jt * TCH + c * 512: jt * TCH + (c + 1) * 512]
                        if e == 0:
                            wt(v, "v", acc_read[jt])
                            i = PV.inc(v.tensor_copy(dst, PB[bn][:]))
                        else:
                            i = PV.inc(v.tensor_tensor(dst, dst, PB[bn][:], ALU.add))
                        _v.out = i
                    VS(_v)
                    rel_banks([BANKS[bn]], (PV, _v.out))
                acc_state_jt[jt] = (PV, _v.out)
                wo_cons[slot] = (PT, PT.n)
            h_cons = (PT, PT.n)

        r2 = [None] * TT
        for jt in range(TT):
            def _v(v, jt=jt, toff=toff):
                wt(v, "v", acc_state_jt[jt])
                _v.out = PV.inc(v.tensor_tensor(ACC[:, jt * TCH:(jt + 1) * TCH],
                                                ACC[:, jt * TCH:(jt + 1) * TCH],
                                                X1T[:, jt * T + toff: jt * T + toff + TCH], ALU.add))
            VS(_v)
            r2[jt] = (PV, _v.out)

        sqd2 = [None, None]
        stats2_val = None
        for jt in range(TT):
            slot = jt % 2

            def _s(s, jt=jt, slot=slot):
                wt(s, "s", r2[jt])
                wt(s, "s", sqd2[slot])
                wt(s, "s", h_cons)
                _s.out = PS.inc(s.activation(SQ2[:, slot * TCH:(slot + 1) * TCH],
                                             ACC[:, jt * TCH:(jt + 1) * TCH], AF.Square))
            SS(_s)
            sq2_val = _s.out

            def _t(t, jt=jt, slot=slot, sq2_val=sq2_val):
                if jt == 0:
                    wait_banks(t, p2_dep_all)
                wt(t, "t", r2[jt])
                wt(t, "t", (PS, sq2_val))
                for c in range(2):
                    t.matmul(P2[:, c * 512:(c + 1) * 512], ONESF[:],
                             ACC[:, jt * TCH + c * 512: jt * TCH + (c + 1) * 512],
                             start=(jt == 0), stop=(jt == TT - 1))
                for c in range(2):
                    mm = t.matmul(P2[:, 1024 + c * 512: 1024 + (c + 1) * 512], ONESB[:],
                                  SQ2[:, slot * TCH + c * 512: slot * TCH + (c + 1) * 512],
                                  start=(jt == 0), stop=(jt == TT - 1))
                _t.out = PT.inc(mm)
            TS(_t)
            sqd2[slot] = (PT, _t.out)
            stats2_val = _t.out

        def _v(v, stats2_val=stats2_val):
            wt(v, "v", (PT, stats2_val))
            i = PV.inc(v.tensor_scalar(LNM2[:], P2[:, 0:1024], 1.0 / T, None, ALU.mult))
            wt(v, "v", (PV, i))
            v.tensor_scalar(P2[:, 1024:2048], P2[:, 1024:2048], 1.0 / T, None, ALU.mult)
            i = PV.inc(v.tensor_tensor(CWB[:], LNM2[:], LNM2[:], ALU.mult))
            wt(v, "v", (PV, i))
            _v.out = PV.inc(v.tensor_tensor(P2[:, 1024:2048], P2[:, 1024:2048],
                                            CWB[:], ALU.subtract))
        VS(_v)
        var2_val = _v.out

        def _s(s, var2_val=var2_val):
            wt(s, "s", (PV, var2_val))
            i = PS.inc(s.activation(P2[:, 1024:2048], P2[:, 1024:2048], AF.Ln, bias=CST[:, 1:2]))
            wt(s, "s", (PS, i))
            _s.out = PS.inc(s.activation(P2[:, 1024:2048], P2[:, 1024:2048], AF.Exp, scale=-0.5, bias=CST[:, 3:4]))
        SS(_s)
        rstd2_val = _s.out

        def _v(v, rstd2_val=rstd2_val):
            wt(v, "v", (PS, rstd2_val))
            _v.out = PV.inc(v.tensor_tensor(LNM2[:], LNM2[:], P2[:, 1024:2048], ALU.mult))
        VS(_v)
        mr2_val = _v.out

        for jt in range(TT):
            slot = jt % 2

            def _v(v, jt=jt, slot=slot, mr2_val=mr2_val):
                wt(v, "v", (PV, mr2_val))
                wt(v, "v", h_cons)
                dep = outs_cons[slot]
                if dep is not None:
                    wt(v, "v", dep)
                i0 = PV.inc(v.tensor_tensor(OUTS[slot][:], ACC[:, jt * TCH:(jt + 1) * TCH],
                                            P2[:, 1024:2048], ALU.mult))
                wt(v, "v", (PV, i0))
                _v.out = PV.inc(v.tensor_tensor(OUTS[slot][:], OUTS[slot][:], LNM2[:], ALU.subtract))
            VS(_v)
            o2_val = _v.out
            acc_read[jt] = (PV, o2_val)

            def _y(y, jt=jt, slot=slot, toff=toff, o2_val=o2_val):
                wt(y, "y", (PV, o2_val))
                y.dma_start(out=outT[jt * 128:(jt + 1) * 128, toff:toff + TCH],
                            in_=OUTS[slot][:]).then_inc(SST[slot].h, 16)
                SST[slot].n += 16
            YS(_y)
            outs_cons[slot] = (SST[slot], SST[slot].n)
        rel_banks(p2_dep_all, (PV, mr2_val))
        rel_banks(["P2c", "P2d"], acc_read[TT - 1])
        half_guard = [(PT, PT.n), (PV, PV.n), (PS, PS.n), (SST[0], SST[0].n), (SST[1], SST[1].n)]

    def _y(y):
        y.wait_ge(SST[0].h, SST[0].n)
        y.wait_ge(SST[1].h, SST[1].n)
    YS(_y)
    ctx.close()
    import concourse.mybir as _mb
    for _alloc in nc.main_func.allocations:
        if isinstance(_alloc, _mb.MemoryLocationSet) and _alloc.memorylocations:
            for _m in _alloc.memorylocations:
                if _m.name.startswith("const-"):
                    assert _m.addr + 64 <= A_OFF, (_m.name, _m.addr)
    return nc, dbg


def _prep_inputs(inputs):
    f32 = np.float32
    x = np.asarray(inputs["L0_attn_x"], dtype=f32)
    base = {
        "wqT": np.ascontiguousarray(np.asarray(inputs["L0_attn_wq"], f32).T),
        "wkT": np.ascontiguousarray(np.asarray(inputs["L0_attn_wk"], f32).T),
        "wvT": np.ascontiguousarray(np.asarray(inputs["L0_attn_wv"], f32).T),
        "woT": np.ascontiguousarray(np.asarray(inputs["L0_attn_wo"], f32).T),
        "gk": np.ascontiguousarray(np.asarray(inputs["L0_ffn_gate_kernel"], f32)),
        "lam": np.concatenate([np.asarray(inputs[k], f32).reshape(1, -1) for k in
                               ("L0_attn_lq1", "L0_attn_lk1", "L0_attn_lq2", "L0_attn_lk2")],
                              axis=1),
        "wi0": np.ascontiguousarray(np.asarray(inputs["L0_ffn_wi_0"], f32)),
        "wi1": np.ascontiguousarray(np.asarray(inputs["L0_ffn_wi_1"], f32)),
        "wom": np.ascontiguousarray(np.asarray(inputs["L0_ffn_wo"], f32)),
        "idf": np.eye(128, dtype=f32),
        "onesb_in": np.ones((128, 128), f32),
        "onesf_in": np.ones((128, 128), f32),
        "maskf": np.where(np.arange(128)[:, None] > np.arange(128)[None, :],
                          f32(-1e30), f32(0.0)).astype(f32),
    }
    cst = np.zeros((128, 8), f32)
    cst[:, 0] = RMS_EPS
    cst[:, 1] = LN_EPS
    cst[:, 2] = -0.5 * np.log(128.0)
    base["cst_in"] = cst
    esel = np.zeros((8, E * 128), f32)
    for e in range(E):
        esel[e, e * 128:(e + 1) * 128] = 1.0
    base["esel_in"] = esel
    in_maps = []
    for b in range(8):
        m = dict(base)
        m["xt"] = np.ascontiguousarray(x[b].T)
        in_maps.append(m)
    return in_maps


def kernel(**inputs):
    in_maps = _prep_inputs(inputs)
    nc, _ = build()
    res = run_bass_kernel_spmd(nc, in_maps, core_ids=list(range(8)))
    out = np.empty((8, T, D), dtype=np.float32)
    for b in range(8):
        out[b] = np.asarray(res.results[b]["outT"], dtype=np.float32).T
    return out
